# version 08
# baseline (speedup 1.0000x reference)
import numpy as np
import concourse.bass as bass
import concourse.mybir as mybir
from concourse.bass_utils import run_bass_kernel_spmd
from concourse.ap import AP

F32 = mybir.dt.float32
BF16 = mybir.dt.bfloat16
AF = mybir.ActivationFunctionType
ALU = mybir.AluOpType

N_CORES = 8
D = 4096
DT = 32
TOK = 2048
CHUNK = 512
NCH = TOK // CHUNK
RANK = 1024
INTER = 11008
FP = 86
L = 4096
EPS = 1e-5
DEBUG = False


def _host_consts():
    jj = np.arange(L, dtype=np.float64)
    ang = (np.float64(-2.0 * np.pi) * (jj / L)).astype(np.float32)
    om = np.cos(ang.astype(np.float64)) + 1j * np.sin(ang.astype(np.float64))
    w = (1.0 - om) / (1.0 + om)
    c = 2.0 / (1.0 + om)
    wflat = np.stack([w.real, w.imag]).astype(np.float32)
    cri = np.concatenate([c.real.reshape(64, 64), c.imag.reshape(64, 64)],
                         axis=1).astype(np.float32)
    k64 = np.arange(64, dtype=np.float64)
    cos64 = np.cos(2 * np.pi * np.outer(k64, k64) / 64.0)
    sin64 = np.sin(2 * np.pi * np.outer(k64, k64) / 64.0)
    w64a = np.concatenate([cos64, -sin64], axis=0).astype(np.float32)
    w64b = np.concatenate([sin64, cos64], axis=0).astype(np.float32)
    w64c = (np.concatenate([cos64, -sin64], axis=0) / L).astype(np.float32)
    twr = np.cos(2 * np.pi * np.outer(k64, k64) / L)
    twi = np.sin(2 * np.pi * np.outer(k64, k64) / L)
    tw64 = np.zeros((128, 128), np.float32)
    tw64[0:64, 0:64] = twr; tw64[64:128, 0:64] = twr
    tw64[0:64, 64:128] = twi; tw64[64:128, 64:128] = twi
    ones1 = np.ones((1, 128), np.float32)
    onesbf = np.ones((128, 1), np.float32)
    sc = np.tile(np.array([[EPS, np.log(2.0), -0.5, -1.0, 1.0 / D]],
                          np.float32), (128, 1))
    return dict(wflat=wflat, cri=cri, w64a=w64a, w64b=w64b, w64c=w64c,
                tw64=tw64, ones1=ones1, onesbf=onesbf, sc=sc)


class Ctr:
    def __init__(self, h):
        self.h = h
        self.n = 0

    def inc(self, k=1):
        self.n += k
        return self.n


class Sched:
    def __init__(self):
        self.items = []

    def add(self, engine, fn):
        self.items.append((engine, fn))

    def replay(self, engine_name, eng):
        for name, fn in self.items:
            if name == engine_name:
                fn(eng)


class Buf:
    def __init__(self, t, w):
        self.t = t
        self.w = w

    def __getitem__(self, key):
        par, rows, cols = key
        if isinstance(cols, slice):
            c0 = cols.start if cols.start is not None else 0
            c1 = cols.stop if cols.stop is not None else self.w
        else:
            raise TypeError(cols)
        return self.t[rows, par * self.w + c0:par * self.w + c1]


def build_program():
    nc = bass.Bass("TRN2", target_bir_lowering=False)
    hc = _host_consts()

    x_d = nc.dram_tensor("x_t", [D, TOK], F32, kind="ExternalInput").ap()
    out_d = nc.dram_tensor("out_t", [D, TOK], F32, kind="ExternalOutput").ap()
    lam_d = nc.dram_tensor("lam", [64, 2], F32, kind="ExternalInput").ap()
    p_d = nc.dram_tensor("p_ri", [64, 2], F32, kind="ExternalInput").ap()
    q_d = nc.dram_tensor("q_ri", [64, 2], F32, kind="ExternalInput").ap()
    ct_d = nc.dram_tensor("ct_ri", [64, 2], F32, kind="ExternalInput").ap()
    vcret_d = nc.dram_tensor("vc_reT", [64, 64], F32, kind="ExternalInput").ap()
    vcimt_d = nc.dram_tensor("vc_imT", [64, 64], F32, kind="ExternalInput").ap()
    bp_d = nc.dram_tensor("bp", [64, 1], F32, kind="ExternalInput").ap()
    dd_d = nc.dram_tensor("dd", [1, 1], F32, kind="ExternalInput").ap()
    ls_d = nc.dram_tensor("ls", [1, 1], F32, kind="ExternalInput").ap()
    u1_d = nc.dram_tensor("u1", [D, RANK], F32, kind="ExternalInput").ap()
    v1_d = nc.dram_tensor("v1", [RANK, 2 * INTER], F32, kind="ExternalInput").ap()
    u2_d = nc.dram_tensor("u2", [INTER, RANK], F32, kind="ExternalInput").ap()
    v2_d = nc.dram_tensor("v2", [RANK, D], F32, kind="ExternalInput").ap()
    b1t_d = nc.dram_tensor("b1t", [128, 2 * FP], F32, kind="ExternalInput").ap()
    b2t_d = nc.dram_tensor("b2t", [128, DT], F32, kind="ExternalInput").ap()

    kpad_d = nc.dram_tensor("kpad", [4224], F32, kind="Internal").ap()
    ksums_dram = nc.dram_tensor("ksums_dram", [8, 4096], F32, kind="Internal").ap()
    u1s_d = nc.dram_tensor("u1s", [D, RANK], BF16, kind="Internal").ap()
    v1s_d = nc.dram_tensor("v1s", [FP, 128, 2048], BF16, kind="Internal").ap()
    u2s_d = nc.dram_tensor("u2s", [INTER, RANK], BF16, kind="Internal").ap()
    v2s_d = nc.dram_tensor("v2s", [16, 128, 2048], BF16, kind="Internal").ap()
    acts_d = nc.dram_tensor("acts", [FP, 128, CHUNK], BF16, kind="Internal").ap()
    dbg_d = (nc.dram_tensor("dbg", [128, 8192], F32, kind="ExternalOutput").ap()
             if DEBUG else None)
    x1d_d = (nc.dram_tensor("x1dump", [128, DT * CHUNK], F32,
                            kind="ExternalOutput").ap() if DEBUG else None)

    wflat_d = nc.inline_tensor(hc["wflat"], "c_wflat").ap()
    cri_d = nc.inline_tensor(hc["cri"], "c_cri").ap()
    w64a_d = nc.inline_tensor(hc["w64a"], "c_w64a").ap()
    w64b_d = nc.inline_tensor(hc["w64b"], "c_w64b").ap()
    w64c_d = nc.inline_tensor(hc["w64c"], "c_w64c").ap()
    tw64_d = nc.inline_tensor(hc["tw64"], "c_tw64").ap()
    ones1_d = nc.inline_tensor(hc["ones1"], "c_ones1").ap()
    onesbf_d = nc.inline_tensor(hc["onesbf"], "c_onesbf").ap()
    sc_d = nc.inline_tensor(hc["sc"], "c_sc").ap()

    import contextlib
    with contextlib.ExitStack() as _st:
        xbuf = _st.enter_context(nc.sbuf_tensor("xbuf", [128, DT * CHUNK], F32))
        ubf = _st.enter_context(nc.sbuf_tensor("ubf", [128, DT * CHUNK], BF16))
        tall = _st.enter_context(nc.sbuf_tensor("tall", [128, 4096], BF16))
        psb = _st.enter_context(nc.sbuf_tensor("psb", [128, 8 * CHUNK], BF16))
        ztsb = _st.enter_context(nc.sbuf_tensor("ztsb", [128, 8 * CHUNK], BF16))
        u1b = Buf(_st.enter_context(nc.sbuf_tensor("u1b", [128, 2 * 1024], BF16)), 1024)
        v1b = Buf(_st.enter_context(nc.sbuf_tensor("v1b", [128, 3 * 2048], BF16)), 2048)
        u2b = Buf(_st.enter_context(nc.sbuf_tensor("u2b", [128, 3 * 1024], BF16)), 1024)
        v2b = Buf(_st.enter_context(nc.sbuf_tensor("v2b", [128, 2 * 2048], BF16)), 2048)
        actw = Buf(_st.enter_context(nc.sbuf_tensor("actw", [128, 3 * CHUNK], BF16)), CHUNK)
        actr = Buf(_st.enter_context(nc.sbuf_tensor("actr", [128, 3 * CHUNK], BF16)), CHUNK)
        sgb = Buf(_st.enter_context(nc.sbuf_tensor("sgb", [128, 2 * CHUNK], BF16)), CHUNK)
        xbft = Buf(_st.enter_context(nc.sbuf_tensor("xbft", [128, 2 * CHUNK], BF16)), CHUNK)
        sqt = Buf(_st.enter_context(nc.sbuf_tensor("sqt", [128, 2 * CHUNK], BF16)), CHUNK)
        ttmp = Buf(_st.enter_context(nc.sbuf_tensor("ttmp", [128, 1 * CHUNK], F32)), CHUNK)
        rtmp = Buf(_st.enter_context(nc.sbuf_tensor("rtmp", [128, 1 * CHUNK], F32)), CHUNK)
        mb_sb = _st.enter_context(nc.sbuf_tensor("mb_sb", [128, CHUNK], F32))
        sb_sb = _st.enter_context(nc.sbuf_tensor("sb_sb", [128, CHUNK], F32))
        lam_s = _st.enter_context(nc.sbuf_tensor("lam_s", [64, 2], F32))
        p_s = _st.enter_context(nc.sbuf_tensor("p_s", [64, 2], F32))
        q_s = _st.enter_context(nc.sbuf_tensor("q_s", [64, 2], F32))
        ct_s = _st.enter_context(nc.sbuf_tensor("ct_s", [64, 2], F32))
        vcret_s = _st.enter_context(nc.sbuf_tensor("vcret_s", [64, 64], F32))
        vcimt_s = _st.enter_context(nc.sbuf_tensor("vcimt_s", [64, 64], F32))
        bp_s = _st.enter_context(nc.sbuf_tensor("bp_s", [64, 1], F32))
        dd_s = _st.enter_context(nc.sbuf_tensor("dd_s", [1, 1], F32))
        ls_s = _st.enter_context(nc.sbuf_tensor("ls_s", [1, 1], F32))
        dsb = _st.enter_context(nc.sbuf_tensor("dsb", [128, 1], F32))
        b1t_s = _st.enter_context(nc.sbuf_tensor("b1t_s", [128, 2 * FP], F32))
        b2t_s = _st.enter_context(nc.sbuf_tensor("b2t_s", [128, DT], F32))
        t2s_s = _st.enter_context(nc.sbuf_tensor("t2s_s", [1, 1], F32))
        cri_s = _st.enter_context(nc.sbuf_tensor("cri_s", [64, 128], F32))
        w64a_s = _st.enter_context(nc.sbuf_tensor("w64a_s", [128, 64], F32))
        w64b_s = _st.enter_context(nc.sbuf_tensor("w64b_s", [128, 64], F32))
        w64c_s = _st.enter_context(nc.sbuf_tensor("w64c_s", [128, 64], F32))
        tw64_s = _st.enter_context(nc.sbuf_tensor("tw64_s", [128, 128], F32))
        ones1_s = _st.enter_context(nc.sbuf_tensor("ones1_s", [1, 128], F32))
        onesbf_s = _st.enter_context(nc.sbuf_tensor("onesbf_s", [128, 1], BF16))
        sc_s = _st.enter_context(nc.sbuf_tensor("sc_s", [128, 5], F32))
        w8 = _st.enter_context(nc.sbuf_tensor("w8", [128, 8], F32))
        wtmp = _st.enter_context(nc.sbuf_tensor("wtmp", [64, 8], F32))
        bri = _st.enter_context(nc.sbuf_tensor("bri", [64, 2], F32))
        invri = Buf(_st.enter_context(nc.sbuf_tensor("invri", [128, 1 * 512], F32)), 512)
        denre = _st.enter_context(nc.sbuf_tensor("denre", [64, 512], F32))
        denim = _st.enter_context(nc.sbuf_tensor("denim", [64, 512], F32))
        sq1 = _st.enter_context(nc.sbuf_tensor("sq1", [64, 512], F32))
        sq2 = _st.enter_context(nc.sbuf_tensor("sq2", [64, 512], F32))
        ksc = _st.enter_context(nc.sbuf_tensor("ksc", [8, 512], F32))
        wrec = _st.enter_context(nc.sbuf_tensor("wrec", [1, 512], F32))
        wimc = _st.enter_context(nc.sbuf_tensor("wimc", [1, 512], F32))
        kst = _st.enter_context(nc.sbuf_tensor("kst", [64, 512], F32))
        arri = _st.enter_context(nc.sbuf_tensor("arri", [128, 64], F32))
        atm = _st.enter_context(nc.sbuf_tensor("atm", [64, 384], F32))
        bri2 = _st.enter_context(nc.sbuf_tensor("bri2", [128, 64], F32))
        twt1 = _st.enter_context(nc.sbuf_tensor("twt1", [128, 64], F32))
        twt2 = _st.enter_context(nc.sbuf_tensor("twt2", [128, 64], F32))
        ksb = _st.enter_context(nc.sbuf_tensor("ksb", [64, 64], F32))
        zrow = _st.enter_context(nc.sbuf_tensor("zrow", [1, 128], F32))
        mrow = _st.enter_context(nc.sbuf_tensor("mrow", [1, CHUNK], F32))
        msrow = _st.enter_context(nc.sbuf_tensor("msrow", [1, CHUNK], F32))
        varrow = _st.enter_context(nc.sbuf_tensor("varrow", [1, CHUNK], F32))
        lnrow = _st.enter_context(nc.sbuf_tensor("lnrow", [1, CHUNK], F32))
        srow = _st.enter_context(nc.sbuf_tensor("srow", [1, CHUNK], F32))
        pb0 = _st.enter_context(nc.psum_tensor("pb0", [128, 512], F32))
        pb1 = _st.enter_context(nc.psum_tensor("pb1", [128, 512], F32))
        pb2 = _st.enter_context(nc.psum_tensor("pb2", [128, 512], F32))
        pb3 = _st.enter_context(nc.psum_tensor("pb3", [128, 512], F32))
        pb4 = _st.enter_context(nc.psum_tensor("pb4", [128, 512], F32))
        pb5 = _st.enter_context(nc.psum_tensor("pb5", [128, 512], F32))
        pb6 = _st.enter_context(nc.psum_tensor("pb6", [128, 512], F32))
        pb7 = _st.enter_context(nc.psum_tensor("pb7", [128, 512], F32))
        h_sync = _st.enter_context(nc.semaphore("s_sync"))
        h_gpd = _st.enter_context(nc.semaphore("s_gpd"))
        _chans = {}
        def mkchan(name, serialize, depth=1):
            h = _st.enter_context(nc.semaphore(name))
            c = Ctr(h)
            c.serialize = serialize
            c.depth = depth
            _chans[name] = c
            return c
        ch_small = mkchan("c_small", False)
        ch_cu1 = mkchan("c_cu1", False)
        ch_cv1 = [mkchan(f"c_cv1_{i}", True, depth=3) for i in range(4)]
        ch_cu2 = [mkchan(f"c_cu2_{i}", True, depth=3) for i in range(2)]
        ch_cv2 = [mkchan(f"c_cv2_{i}", True, depth=3) for i in range(2)]
        ch_kp = mkchan("c_kp", True)
        ch_tall = mkchan("c_tall", False)
        ch_x = [mkchan(f"c_x{q}", True) for q in range(4)]
        ch_u1b = [mkchan(f"c_u1b{i}", True) for i in range(2)]
        ch_v1b = [mkchan(f"c_v1b{i}", True) for i in range(3)]
        ch_u2b = [mkchan(f"c_u2b{i}", True) for i in range(3)]
        ch_actr = [mkchan(f"c_actr{i}", True) for i in range(3)]
        ch_v2b = [mkchan(f"c_v2b{i}", True) for i in range(2)]
        ch_actw = [mkchan(f"c_actw{i}", True) for i in range(3)]
        ch_out = [mkchan(f"c_out{q}", True) for q in range(4)]
        ch_dbg = mkchan("c_dbg", True)
        h_pe = _st.enter_context(nc.semaphore("s_pe"))
        h_dve = _st.enter_context(nc.semaphore("s_dve"))
        h_sc = _st.enter_context(nc.semaphore("s_sc"))
        h_gp = _st.enter_context(nc.semaphore("s_gp"))
        block = _st.enter_context(nc.Block())

        pb = [pb0, pb1, pb2, pb3, pb4, pb5, pb6, pb7]
        S = Sched()
        c_sync = Ctr(h_sync)
        c_gpd = Ctr(h_gpd)
        c_pe = Ctr(h_pe)
        c_dve = Ctr(h_dve)
        c_sc = Ctr(h_sc)
        c_gp = Ctr(h_gp)

        def dma(qname, chan, out_ap, in_ap, wait=None):
            idx = chan.n // 16
            tgt = chan.inc(16)
            ser = getattr(chan, "serialize", False)
            depth = getattr(chan, "depth", 1)
            def f(eng, out_ap=out_ap, in_ap=in_ap, wait=wait, idx=idx):
                if ser and idx >= depth:
                    eng.wait_ge(chan.h, 16 * (idx - depth + 1))
                if wait:
                    for (h, v) in wait:
                        eng.wait_ge(h, v)
                eng.dma_start(out=out_ap, in_=in_ap).then_inc(chan.h, 16)
            S.add(qname, f)
            return (chan.h, tgt)

        def ins(ename, ctr, fn, wait=None):
            tgt = ctr.inc(1)
            def f(eng, fn=fn, wait=wait):
                if wait:
                    for (h, v) in wait:
                        eng.wait_ge(h, v)
                fn(eng).then_inc(ctr.h, 1)
            S.add(ename, f)
            return tgt

        def dbg(col0, src_ap, wait):
            if not DEBUG:
                return
            rows, cols = src_ap.shape[0], src_ap.shape[1]
            dma("gpsimd", c_gpd,
                AP(dbg_d.tensor, col0, [[4096, rows], [1, cols]]), src_ap,
                wait=wait)

        def mm_group(mms, wait=None, skip_check=False):
            tgt = c_pe.inc(1)
            def f(eng, mms=mms, wait=wait):
                if wait:
                    for (h, v) in wait:
                        eng.wait_ge(h, v)
                last = None
                for (o, lf, rf, st, sp) in mms:
                    last = eng.matmul(o, lf(), rf(), start=st, stop=sp,
                                      skip_group_check=skip_check)
                last.then_inc(h_pe, 1)
            S.add("tensor", f)
            return tgt

        g_small = None
        for dst, src in [
            (lam_s[:], lam_d), (p_s[:], p_d), (q_s[:], q_d), (ct_s[:], ct_d),
            (vcret_s[:], vcret_d), (vcimt_s[:], vcimt_d), (bp_s[:], bp_d),
            (dd_s[:], dd_d), (ls_s[:], ls_d),
            (b1t_s[:], b1t_d), (b2t_s[:], b2t_d),
            (cri_s[:], cri_d),
            (w64a_s[:], w64a_d), (w64b_s[:], w64b_d), (w64c_s[:], w64c_d),
            (tw64_s[:], tw64_d), (ones1_s[:], ones1_d), (onesbf_s[:], onesbf_d),
            (sc_s[:], sc_d),
        ]:
            g_small = dma("gpsimd", ch_small, dst, src)

        t_sc1 = ins("scalar", c_sc, lambda e: e.activation(
            t2s_s[0:1, 0:1], ls_s[:], AF.Exp,
            scale=sc_s[0:1, 3:4], bias=sc_s[0:1, 1:2]),
            wait=[g_small])
        t_bre = mm_group([(pb7[0:64, 0:1], lambda: vcret_s[:],
                           lambda: bp_s[:], True, True)],
                         wait=[g_small])
        t_bim = mm_group([(pb7[0:64, 1:2], lambda: vcimt_s[:],
                           lambda: bp_s[:], True, True)], skip_check=True)
        t_evb = ins("scalar", c_sc, lambda e: e.copy(bri[:], pb7[0:64, 0:2]),
                    wait=[(h_pe, t_bim)])
        t_bd = mm_group([(pb5[:, 0:1], lambda: ones1_s[0:1, :],
                          lambda: dd_s[:], True, True)])
        t_evd = ins("scalar", c_sc, lambda e: e.copy(dsb[:], pb5[:, 0:1]),
                    wait=[(h_pe, t_bd)])

        wpairs = [(ct_s, bri), (ct_s, p_s), (q_s, bri), (q_s, p_s)]

        def wprods(e):
            last = None
            for k, (a, b) in enumerate(wpairs):
                last = e.tensor_tensor(sq1[0:64, 4 * k:4 * k + 1],
                                       a[:, 0:1], b[:, 0:1], ALU.mult)
                last = e.tensor_tensor(sq1[0:64, 4 * k + 1:4 * k + 2],
                                       a[:, 1:2], b[:, 1:2], ALU.mult)
                last = e.tensor_tensor(sq1[0:64, 4 * k + 2:4 * k + 3],
                                       a[:, 0:1], b[:, 1:2], ALU.mult)
                last = e.tensor_tensor(sq1[0:64, 4 * k + 3:4 * k + 4],
                                       a[:, 1:2], b[:, 0:1], ALU.mult)
            return last
        t_wp = ins("vector", c_dve, wprods, wait=[(h_sc, t_evb)])

        def wcombine(e):
            last = None
            for k in range(4):
                last = e.tensor_tensor(wtmp[:, 2 * k:2 * k + 1],
                                       sq1[0:64, 4 * k:4 * k + 1],
                                       sq1[0:64, 4 * k + 1:4 * k + 2], ALU.add)
                last = e.tensor_tensor(wtmp[:, 2 * k + 1:2 * k + 2],
                                       sq1[0:64, 4 * k + 2:4 * k + 3],
                                       sq1[0:64, 4 * k + 3:4 * k + 4],
                                       ALU.subtract)
            return last
        t_w = ins("vector", c_dve, wcombine, wait=[(h_dve, t_wp)])

        def build_w8(e):
            last = None
            for k in range(4):
                wr = wtmp[:, 2 * k:2 * k + 1]
                wi = wtmp[:, 2 * k + 1:2 * k + 2]
                e.tensor_copy(w8[0:64, 2 * k:2 * k + 1], wr)
                e.tensor_scalar(w8[64:128, 2 * k:2 * k + 1], wi, -1.0, None,
                                ALU.mult)
                e.tensor_copy(w8[0:64, 2 * k + 1:2 * k + 2], wi)
                last = e.tensor_copy(w8[64:128, 2 * k + 1:2 * k + 2], wr)
            return last
        t_w8 = ins("vector", c_dve, build_w8, wait=[(h_dve, t_w)])

        prev_ev = 0
        prev_dve = 0
        prev_kdma = 0
        prev_cau = 0
        prev_g2 = 0
        for jc in range(8):
            sl = slice(jc * 512, (jc + 1) * 512)
            par = jc % 2
            t_wre = dma("gpsimd", ch_kp, wrec[:], wflat_d[0:1, sl])
            t_wim = dma("gpsimd", ch_kp, wimc[:], wflat_d[1:2, sl])
            wgc = [t_wim, (h_sc, t_sc1)]
            if jc >= 1:
                wgc.append((h_pe, prev_g2))
            ins("vector", c_dve, lambda e: e.tensor_scalar(
                wrec[:], wrec[:], t2s_s[0:1, 0:1], None, ALU.mult), wait=wgc)
            t_gch = ins("vector", c_dve, lambda e: e.tensor_scalar(
                wimc[:], wimc[:], t2s_s[0:1, 0:1], None, ALU.mult))
            wg = [(h_dve, t_gch)] if jc == 0 else [(h_dve, prev_dve),
                                                   (h_dve, t_gch)]
            t_g1 = mm_group([(pb0[0:64, :], lambda: ones1_s[0:1, 0:64],
                              lambda: wrec[:], True, True)],
                            wait=wg, skip_check=True)
            t_g2 = mm_group([(pb1[0:64, :], lambda: ones1_s[0:1, 0:64],
                              lambda: wimc[:], True, True)],
                            skip_check=True)
            prev_g2 = t_g2
            ins("vector", c_dve, lambda e: e.tensor_scalar(
                denre[:], pb0[0:64, :], lam_s[:, 0:1], None, ALU.subtract),
                wait=[(h_pe, t_g2)])
            ins("vector", c_dve, lambda e: e.tensor_scalar(
                denim[:], pb1[0:64, :], lam_s[:, 1:2], None, ALU.subtract))
            ins("vector", c_dve, lambda e: e.tensor_tensor(
                sq1[:], denre[:], denre[:], ALU.mult))
            ins("vector", c_dve, lambda e: e.tensor_tensor(
                sq2[:], denim[:], denim[:], ALU.mult))
            t_mag = ins("vector", c_dve, lambda e: e.tensor_tensor(
                kst[:], sq1[:], sq2[:], ALU.add))
            t_rec = ins("vector", c_dve, lambda e: e.reciprocal(
                sq2[:], kst[:]), wait=[(h_dve, t_mag)])
            wir = [(h_dve, t_rec)]
            if jc >= 1:
                wir.append((h_pe, prev_cau))
            ins("vector", c_dve, lambda e: e.tensor_tensor(
                invri[0, 0:64, :], denre[:], sq2[:], ALU.mult), wait=wir)
            t_ii = ins("vector", c_dve, lambda e: e.scalar_tensor_tensor(
                invri[0, 64:128, :], denim[:], -1.0, sq2[:], ALU.mult,
                ALU.mult))
            prev_dve = t_ii
            wc = [(h_dve, t_ii)]
            if jc == 0:
                wc.append((h_dve, t_w8))
            else:
                wc.append((h_sc, prev_ev))
            t_cau = mm_group([(pb2[0:8, :], lambda: w8[:],
                               lambda: invri[0, :, :], True, True)],
                             wait=wc, skip_check=True)
            prev_cau = t_cau
            wev = [(h_pe, t_cau)]
            if jc >= 1:
                wev.append(prev_kdma)
            prev_ev = ins("scalar", c_sc, lambda e: e.copy(
                ksc[:], pb2[0:8, :]), wait=wev)
            prev_kdma = dma("gpsimd", ch_kp,
                            AP(ksums_dram.tensor, jc * 512, [[4096, 8], [1, 512]]),
                            ksc[:], wait=[(h_sc, prev_ev)])
        t_kdram = prev_kdma
        t_kst = None
        for r in range(8):
            t_kst = dma("gpsimd", ch_kp, kst[:, r * 64:(r + 1) * 64],
                        AP(ksums_dram.tensor, r * 4096, [[64, 64], [1, 64]]),
                        wait=[t_kdram] if r == 0 else None)

        KB = lambda r: kst[:, r * 64:(r + 1) * 64]
        A0, A1, A2, A3, A4, A5 = (atm[:, i * 64:(i + 1) * 64] for i in range(6))

        def at_roots(e):
            e.tensor_scalar(A0[:], KB(6), 1.0, None, ALU.add)
            e.tensor_tensor(A1[:], KB(2), KB(4), ALU.mult)
            e.tensor_tensor(A2[:], KB(3), KB(5), ALU.mult)
            e.tensor_tensor(A1[:], A1[:], A2[:], ALU.subtract)
            e.tensor_tensor(A2[:], KB(2), KB(5), ALU.mult)
            e.tensor_tensor(A3[:], KB(3), KB(4), ALU.mult)
            e.tensor_tensor(A2[:], A2[:], A3[:], ALU.add)
            e.tensor_tensor(A3[:], A0[:], A0[:], ALU.mult)
            e.tensor_tensor(A4[:], KB(7), KB(7), ALU.mult)
            e.tensor_tensor(A3[:], A3[:], A4[:], ALU.add)
            e.reciprocal(A3[:], A3[:])
            e.tensor_tensor(A4[:], A1[:], A0[:], ALU.mult)
            e.tensor_tensor(A5[:], A2[:], KB(7), ALU.mult)
            e.tensor_tensor(A4[:], A4[:], A5[:], ALU.add)
            e.tensor_tensor(A4[:], A4[:], A3[:], ALU.mult)
            e.tensor_tensor(A5[:], A2[:], A0[:], ALU.mult)
            e.tensor_tensor(A0[:], A1[:], KB(7), ALU.mult)
            e.tensor_tensor(A5[:], A5[:], A0[:], ALU.subtract)
            e.tensor_tensor(A5[:], A5[:], A3[:], ALU.mult)
            e.tensor_tensor(A4[:], KB(0), A4[:], ALU.subtract)
            e.tensor_tensor(A5[:], KB(1), A5[:], ALU.subtract)
            e.tensor_tensor(A0[:], cri_s[:, 0:64], A4[:], ALU.mult)
            e.tensor_tensor(A1[:], cri_s[:, 64:128], A5[:], ALU.mult)
            e.tensor_tensor(arri[0:64, :], A0[:], A1[:], ALU.subtract)
            e.tensor_tensor(A0[:], cri_s[:, 0:64], A5[:], ALU.mult)
            e.tensor_tensor(A1[:], cri_s[:, 64:128], A4[:], ALU.mult)
            return e.tensor_tensor(arri[64:128, :], A0[:], A1[:], ALU.add)
        t_ar = ins("vector", c_dve, at_roots, wait=[t_kst])

        t_fa = mm_group([(pb3[0:64, 0:64], lambda: arri[:],
                          lambda: w64a_s[:], True, True)],
                        wait=[(h_dve, t_ar)], skip_check=True)
        t_fb = mm_group([(pb4[0:64, 0:64], lambda: arri[:],
                          lambda: w64b_s[:], True, True)], skip_check=True)

        def mk_hi(o, rf):
            def f(eng):
                return eng.matmul(o, arri[:], rf(), start=True, stop=True,
                                  tile_position=(0, 64), skip_group_check=True)
            return f
        t_fah = ins("tensor", c_pe, mk_hi(pb3[64:128, 0:64], lambda: w64a_s[:]))
        t_fbh = ins("tensor", c_pe, mk_hi(pb4[64:128, 0:64], lambda: w64b_s[:]))

        def twiddle(e):
            e.tensor_tensor(twt1[0:64, :], pb3[0:64, 0:64],
                            tw64_s[0:64, 0:64], ALU.mult)
            e.tensor_tensor(twt2[0:64, :], pb4[0:64, 0:64],
                            tw64_s[0:64, 64:128], ALU.mult)
            e.tensor_tensor(bri2[0:64, :], twt1[0:64, :], twt2[0:64, :],
                            ALU.subtract)
            e.tensor_tensor(twt1[64:128, :], pb3[64:128, 0:64],
                            tw64_s[64:128, 64:128], ALU.mult)
            e.tensor_tensor(twt2[64:128, :], pb4[64:128, 0:64],
                            tw64_s[64:128, 0:64], ALU.mult)
            return e.tensor_tensor(bri2[64:128, :], twt1[64:128, :],
                                   twt2[64:128, :], ALU.add)
        t_tw = ins("vector", c_dve, twiddle, wait=[(h_pe, t_fbh)])
        t_fk = mm_group([(pb3[0:64, 64:128], lambda: w64c_s[:],
                          lambda: bri2[:], True, True)],
                        wait=[(h_dve, t_tw)], skip_check=True)
        t_evk = ins("scalar", c_sc, lambda e: e.copy(ksb[:], pb3[0:64, 64:128]),
                    wait=[(h_pe, t_fk)])
        t_z = ins("vector", c_dve, lambda e: e.memset(zrow[:], 0.0))
        t_zd = dma("gpsimd", ch_kp, AP(kpad_d.tensor, 0, [[1, 1], [1, 127]]),
                   zrow[0:1, 0:127], wait=[(h_dve, t_z)])
        t_kd = dma("gpsimd", ch_kp, AP(kpad_d.tensor, 127, [[64, 64], [1, 64]]),
                   ksb[:], wait=[(h_sc, t_evk)])
        t_kd2 = dma("gpsimd", ch_kp, AP(kpad_d.tensor, 4223, [[1, 1], [1, 1]]),
                    zrow[0:1, 0:1])
        for tau in range(128):
            dma("gpsimd", ch_tall, tall[tau:tau + 1, :],
                         AP(kpad_d.tensor, 127 - tau, [[1, 1], [1, 4096]]),
                         wait=[t_kd2] if tau == 0 else None)

        u1_cast_tgt = None
        for i in range(4):
            u1_cast_tgt = dma("gpsimd", ch_cu1,
                              u1s_d[i * 1024:(i + 1) * 1024, :],
                              u1_d[i * 1024:(i + 1) * 1024, :])
        v1_cast_tgt = []
        for f in range(FP):
            src_g = AP(v1_d.tensor, f * 128,
                       [[2 * INTER, 128], [128 * 2 * INTER, 8], [1, 128]])
            dma("gpsimd", ch_cv1[f % 4], v1s_d[f, :, 0:1024], src_g)
            src_u = AP(v1_d.tensor, INTER + f * 128,
                       [[2 * INTER, 128], [128 * 2 * INTER, 8], [1, 128]])
            v1_cast_tgt.append(dma("gpsimd", ch_cv1[f % 4],
                                   v1s_d[f, :, 1024:2048], src_u))
        u2_cast_tgt = []
        for f in range(FP):
            r0 = f * 128
            u2_cast_tgt.append(dma("gpsimd", ch_cu2[f % 2],
                                   u2s_d[r0:r0 + 128, :],
                                   u2_d[r0:r0 + 128, :]))
        v2_cast_tgt = []
        for mg in range(16):
            src = AP(v2_d.tensor, mg * 256, [[D, 128], [128 * D, 8], [1, 256]])
            v2_cast_tgt.append(dma("gpsimd", ch_cv2[mg % 2], v2s_d[mg, :, :], src))


        t_tall = (ch_tall.h, ch_tall.n)

        def overshoot(tgts, nch, depth, chans):
            out = []
            for i in range(len(tgts)):
                j = i + nch * depth
                if j < len(tgts):
                    out.append(tgts[j])
                else:
                    c = chans[i % nch]
                    out.append((c.h, c.n))
            return out
        v1_cast_w = overshoot(v1_cast_tgt, 4, 3, ch_cv1)
        u2_cast_w = overshoot(u2_cast_tgt, 2, 3, ch_cu2)
        v2_cast_w = overshoot(v2_cast_tgt, 2, 3, ch_cv2)
        cast_u1_done = (ch_cu1.h, ch_cu1.n)

        XB = lambda dt: xbuf[:, dt * CHUNK:(dt + 1) * CHUNK]
        UB = lambda dt: ubf[:, dt * CHUNK:(dt + 1) * CHUNK]

        prev_out = None
        prev_actr = None
        for ch in range(NCH):
            tok0 = ch * CHUNK
            first = (ch == 0)

            x_tgt = []
            for qd in range(4):
                src = AP(x_d.tensor, (qd * 8 * 128) * TOK + tok0,
                         [[TOK, 128], [128 * TOK, 8], [1, CHUNK]])
                dst = xbuf[:, qd * 8 * CHUNK:(qd + 1) * 8 * CHUNK] \
                    .rearrange("p (dt c) -> p dt c", dt=8)
                wl = [prev_out[qd]] if prev_out is not None else None
                x_tgt.append(dma("sync", ch_x[qd], dst, src, wait=wl))

            def emit_ln(wait_dt_ready, extra_first_wait):
                stat_tgts = [None] * DT
                for dt in range(DT):
                    par = dt % 2
                    w = list(wait_dt_ready(dt))
                    if dt >= 2:
                        w.append((h_pe, stat_tgts[dt - 2]))
                    t1 = ins("scalar", c_sc, lambda e, dt=dt, par=par: e.copy(
                        xbft[par, :, :], XB(dt)), wait=w)
                    t2 = ins("vector", c_dve, lambda e, dt=dt, par=par:
                             e.tensor_tensor(sqt[par, :, :], XB(dt), XB(dt),
                                             ALU.mult), wait=w)
                    wmm = [(h_sc, t1), (h_dve, t2)]
                    if dt == 0 and extra_first_wait:
                        wmm += extra_first_wait
                    stat_tgts[dt] = mm_group([
                        (pb6[0:1, :], lambda: onesbf_s[:],
                         lambda par=par: xbft[par, :, :], dt == 0, dt == DT - 1),
                        (pb7[0:1, :], lambda: onesbf_s[:],
                         lambda par=par: sqt[par, :, :], dt == 0, dt == DT - 1),
                    ], wait=wmm, skip_check=True)
                t_mean = ins("vector", c_dve, lambda e: e.tensor_scalar(
                    mrow[:], pb6[0:1, :], 1.0 / D, None, ALU.mult),
                    wait=[(h_pe, stat_tgts[DT - 1])])
                ins("vector", c_dve, lambda e: e.tensor_tensor(
                    msrow[:], mrow[:], mrow[:], ALU.mult))
                t_var = ins("vector", c_dve, lambda e: e.scalar_tensor_tensor(
                    varrow[:], pb7[0:1, :], 1.0 / D, msrow[:], ALU.mult,
                    ALU.subtract))
                ins("scalar", c_sc, lambda e: e.activation(
                    lnrow[:], varrow[:], AF.Ln, bias=sc_s[0:1, 0:1]),
                    wait=[(h_dve, t_var)])
                t_rstd = ins("scalar", c_sc, lambda e: e.activation(
                    srow[:], lnrow[:], AF.Exp, scale=sc_s[0:1, 2:3]))
                t_mb = mm_group([(pb6[:, :], lambda: ones1_s[:],
                                  lambda: mrow[:], True, True)],
                                wait=[(h_dve, t_mean), (h_dve, t_var)],
                                skip_check=True)
                t_sb = mm_group([(pb7[:, :], lambda: ones1_s[:],
                                  lambda: srow[:], True, True)],
                                wait=[(h_sc, t_rstd)], skip_check=True)
                ev_m = ins("scalar", c_sc, lambda e: e.copy(mb_sb[:], pb6[:, :]),
                           wait=[(h_pe, t_mb)])
                ev_s = ins("scalar", c_sc, lambda e: e.copy(sb_sb[:], pb7[:, :]),
                           wait=[(h_pe, t_sb)])
                u_tgts = [None] * DT
                for dt in range(DT):
                    w = [(h_sc, ev_s)]
                    ins("vector", c_dve, lambda e, dt=dt: e.tensor_tensor(
                        ttmp[0, :, :], XB(dt), mb_sb[:], ALU.subtract),
                        wait=w)
                    u_tgts[dt] = ins("vector", c_dve, lambda e, dt=dt:
                                     e.tensor_tensor(UB(dt), ttmp[0, :, :],
                                                     sb_sb[:], ALU.mult))
                return u_tgts, ev_s

            u_tgts, ln1_evs = emit_ln(lambda dt: [x_tgt[dt // 8]],
                                      [g_small] if first else None)

            conv_tgts = [None] * DT
            ep_v3 = [None] * DT
            ep_v4 = [None] * DT

            def emit_conv(i):
                bank = pb[i % 4]
                w = [(h_dve, u_tgts[i])]
                if first and i == 0:
                    w.append(t_tall)
                if i >= 4:
                    w.append((h_dve, ep_v3[i - 4]))
                mms = []
                for jj in range(i + 1):
                    d = i - jj
                    mms.append((bank[:, :],
                                (lambda d=d: tall[:, d * 128:(d + 1) * 128]),
                                (lambda jj=jj: UB(jj)),
                                jj == 0, jj == i))
                conv_tgts[i] = mm_group(mms, wait=w, skip_check=True)

            def emit_ep(i):
                wv = [(h_pe, conv_tgts[i])]
                if first and i == 0:
                    wv.append((h_sc, t_evd))
                ep_v3[i] = ins("vector", c_dve, lambda e, i=i:
                               e.tensor_tensor(rtmp[0, :, :], pb[i % 4][:, :],
                                               XB(i), ALU.add), wait=wv)
                ep_v4[i] = ins("vector", c_dve, lambda e, i=i:
                               e.scalar_tensor_tensor(XB(i), UB(i), dsb[:, 0:1],
                                                      rtmp[0, :, :], ALU.mult,
                                                      ALU.add))

            for i in range(DT):
                if i >= 4:
                    emit_ep(i - 4)
                emit_conv(i)
            for i in range(DT - 4, DT):
                emit_ep(i)

            if DEBUG and ch == 0:
                dma("sync", ch_dbg, x1d_d, xbuf[:, :],
                    wait=[(h_dve, ep_v4[DT - 1])])

            v_tgts, ln2_evs = emit_ln(lambda dt: [(h_dve, ep_v4[dt])], None)

            p_mm = [None] * DT
            u1_tgt = [None] * DT
            u1_tgt[0] = dma("sync", ch_u1b[0], u1b[0, :, :], u1s_d[0:128, :],
                            wait=[cast_u1_done] if first else None)
            u1_tgt[1] = dma("sync", ch_u1b[1], u1b[1, :, :], u1s_d[128:256, :],
                            wait=[cast_u1_done] if first else None)
            for k in range(DT):
                par = k % 2
                w = [u1_tgt[k], (h_dve, v_tgts[k])]
                if k == 0:
                    w.append((h_sc, ln2_evs))
                mms = []
                for m in range(8):
                    mms.append((pb[m][:, :],
                                (lambda par=par, m=m:
                                 u1b[par, :, m * 128:(m + 1) * 128]),
                                (lambda k=k: UB(k)),
                                k == 0, k == DT - 1))
                p_mm[k] = mm_group(mms, wait=w, skip_check=True)
                if k + 2 < DT:
                    u1_tgt[k + 2] = dma("sync", ch_u1b[(k + 2) % 2],
                                        u1b[(k + 2) % 2, :, :],
                                        u1s_d[(k + 2) * 128:(k + 3) * 128, :],
                                        wait=[(h_pe, p_mm[k])])
            p_ev = []
            for m in range(8):
                p_ev.append(ins("scalar", c_sc, lambda e, m=m: e.copy(
                    psb[:, m * CHUNK:(m + 1) * CHUNK], pb[m][:, :]),
                    wait=[(h_pe, p_mm[DT - 1])] if m == 0 else None))
            t_pev = p_ev[-1]
            u2_tgt = [None] * FP
            actr_tgt = [None] * FP
            a_sc = [None] * FP
            a_dve = [None] * FP
            a_actdma = [None] * FP
            a_mm_g = [None] * FP
            a_mm_u = [None] * FP
            v1_tgt = [None] * FP
            for f0 in range(3):
                v1_tgt[f0] = dma("sync", ch_v1b[f0], v1b[f0, :, :],
                                 v1s_d[f0, :, :],
                                 wait=[v1_cast_w[f0]] if first else None)
            for f in range(FP):
                par3 = f % 3
                par2 = f % 2
                par4 = f % 3
                gbank = pb[(f % 4) * 2]
                ubank = pb[(f % 4) * 2 + 1]
                w = [v1_tgt[f]]
                if f == 0:
                    w.append((h_sc, t_pev))
                if f >= 4:
                    w.append((h_sc, a_sc[f - 4]))
                    w.append((h_dve, a_dve[f - 4]))
                mmsg = []
                for k2 in range(8):
                    mmsg.append((gbank[:, :],
                                 (lambda par3=par3, k2=k2:
                                  v1b[par3, :, k2 * 128:(k2 + 1) * 128]),
                                 (lambda k2=k2:
                                  psb[:, k2 * CHUNK:(k2 + 1) * CHUNK]),
                                 k2 == 0, k2 == 7))
                a_mm_g[f] = mm_group(mmsg, wait=w, skip_check=True)
                mmsu = []
                for k2 in range(8):
                    mmsu.append((ubank[:, :],
                                 (lambda par3=par3, k2=k2:
                                  v1b[par3, :, 1024 + k2 * 128:1024 + (k2 + 1) * 128]),
                                 (lambda k2=k2:
                                  psb[:, k2 * CHUNK:(k2 + 1) * CHUNK]),
                                 k2 == 0, k2 == 7))
                a_mm_u[f] = mm_group(mmsu, skip_check=True)
                if f + 3 < FP:
                    wl = [(h_pe, a_mm_u[f])]
                    if first:
                        wl.append(v1_cast_w[f + 3])
                    v1_tgt[f + 3] = dma("sync", ch_v1b[(f + 3) % 3],
                                        v1b[(f + 3) % 3, :, :],
                                        v1s_d[f + 3, :, :], wait=wl)
                wsc = [(h_pe, a_mm_g[f])]
                if f >= 2:
                    wsc.append((h_dve, a_dve[f - 2]))
                a_sc[f] = ins("scalar", c_sc, lambda e, f=f, par2=par2,
                              gbank=gbank: e.activation(
                    sgb[par2, :, :], gbank[:, :], AF.Silu,
                    bias=b1t_s[:, f:f + 1]), wait=wsc)
                wdv = [(h_pe, a_mm_u[f]), (h_sc, a_sc[f])]
                if f >= 3:
                    wdv.append(a_actdma[f - 3])
                a_dve[f] = ins("vector", c_dve, lambda e, f=f, par2=par2,
                               par4=par4, ubank=ubank: e.scalar_tensor_tensor(
                    actw[par4, :, :], ubank[:, :],
                    b1t_s[:, FP + f:FP + f + 1], sgb[par2, :, :],
                    ALU.add, ALU.mult), wait=wdv)
                wact = [(h_dve, a_dve[f])]
                if prev_actr is not None:
                    wact.append(prev_actr[f])
                a_actdma[f] = dma("sync", ch_actw[par4], acts_d[f, :, :],
                                  actw[par4, :, :], wait=wact)
                if f == 50:
                    for f0 in range(3):
                        wl = [u2_cast_w[f0]] if first else None
                        u2_tgt[f0] = dma("sync", ch_u2b[f0], u2b[f0, :, :],
                                         u2s_d[f0 * 128:(f0 + 1) * 128, :],
                                         wait=wl)
                        actr_tgt[f0] = dma("sync", ch_actr[f0], actr[f0, :, :],
                                           acts_d[f0, :, :],
                                           wait=[a_actdma[f0]])

            b_mm = [None] * FP
            v2_tgt = [None] * 16
            for g0 in range(2):
                v2_tgt[g0] = dma("sync", ch_v2b[g0], v2b[g0, :, :],
                                 v2s_d[g0, :, :],
                                 wait=[v2_cast_w[g0]] if first else None)
            for f in range(FP):
                par3 = f % 3
                w = [actr_tgt[f], u2_tgt[f]]
                if f == 0:
                    w.append((h_sc, a_sc[FP - 1]))
                    w.append((h_dve, a_dve[FP - 1]))
                mms = []
                for m in range(8):
                    mms.append((pb[m][:, :],
                                (lambda par3=par3, m=m:
                                 u2b[par3, :, m * 128:(m + 1) * 128]),
                                (lambda par3=par3: actr[par3, :, :]),
                                f == 0, f == FP - 1))
                b_mm[f] = mm_group(mms, wait=w, skip_check=True)
                if f + 3 < FP:
                    wl = [(h_pe, b_mm[f])]
                    if first:
                        wl.append(u2_cast_w[f + 3])
                    u2_tgt[f + 3] = dma("sync", ch_u2b[(f + 3) % 3],
                                        u2b[(f + 3) % 3, :, :],
                                        u2s_d[(f + 3) * 128:(f + 4) * 128, :],
                                        wait=wl)
                    actr_tgt[f + 3] = dma("sync", ch_actr[(f + 3) % 3],
                                          actr[(f + 3) % 3, :, :],
                                          acts_d[f + 3, :, :],
                                          wait=[a_actdma[f + 3], (h_pe, b_mm[f])])
            z_ev = []
            for m in range(8):
                z_ev.append(ins("scalar", c_sc, lambda e, m=m: e.copy(
                    ztsb[:, m * CHUNK:(m + 1) * CHUNK], pb[m][:, :]),
                    wait=[(h_pe, b_mm[FP - 1])] if m == 0 else None))
            t_zev = z_ev[-1]
            if DEBUG and ch == 0:
                dma("sync", ch_dbg, AP(dbg_d.tensor, 2048, [[8192, 128], [1, 2048]]),
                    ztsb[:, :].bitcast(F32), wait=[(h_sc, t_zev)])
                acts_f32view = AP(acts_d.tensor, 0,
                                  [[512, 128], [65536, 4], [1, 512]]).bitcast(F32)
                dma("sync", ch_dbg,
                    AP(dbg_d.tensor, 4096, [[8192, 128], [256, 4], [1, 256]]),
                    acts_f32view, wait=[(h_sc, t_zev)])

            NG2 = 16
            v2_mm = [None] * NG2
            v2_dve = [None] * DT
            for g in range(NG2):
                par = g % 2
                base = par * 2
                w = [v2_tgt[g]]
                if g == 0:
                    w.append((h_sc, t_zev))
                if g >= 2:
                    w.append((h_dve, v2_dve[(g - 2) * 2 + 1]))
                mms = []
                for k in range(8):
                    for mi in range(2):
                        mms.append((pb[base + mi][:, :],
                                    (lambda par=par, k=k, mi=mi:
                                     v2b[par, :, k * 256 + mi * 128:k * 256 + (mi + 1) * 128]),
                                    (lambda k=k: ztsb[:, k * CHUNK:(k + 1) * CHUNK]),
                                    k == 0, k == 7))
                v2_mm[g] = mm_group(mms, wait=w, skip_check=True)
                for mi in range(2):
                    m = g * 2 + mi
                    v2_dve[m] = ins("vector", c_dve, lambda e, m=m,
                                    b=pb[base + mi]: e.scalar_tensor_tensor(
                        XB(m), b[:, :], b2t_s[:, m:m + 1], XB(m),
                        ALU.add, ALU.add),
                        wait=[(h_pe, v2_mm[g]), (h_dve, ep_v4[m])])
                if g + 2 < NG2:
                    wl = [(h_pe, v2_mm[g])]
                    if first:
                        wl.append(v2_cast_w[g + 2])
                    v2_tgt[g + 2] = dma("sync", ch_v2b[par], v2b[par, :, :],
                                        v2s_d[g + 2, :, :], wait=wl)
            out_tgt = []
            for qd in range(4):
                dst = AP(out_d.tensor, (qd * 8 * 128) * TOK + tok0,
                         [[TOK, 128], [128 * TOK, 8], [1, CHUNK]])
                src = xbuf[:, qd * 8 * CHUNK:(qd + 1) * 8 * CHUNK] \
                    .rearrange("p (dt c) -> p dt c", dt=8)
                out_tgt.append(dma("sync", ch_out[qd], dst, src,
                                   wait=[(h_dve, v2_dve[qd * 8 + 7]),
                                         x_tgt[qd]]))
            prev_out = out_tgt
            prev_actr = list(actr_tgt)


        chan_totals = [(c.h, c.n) for c in _chans.values() if c.n > 0]

        @block.sync
        def _(eng):
            S.replay("sync", eng)
            for (h, v) in chan_totals:
                eng.wait_ge(h, v)

        @block.tensor
        def _(eng):
            S.replay("tensor", eng)

        @block.vector
        def _(eng):
            S.replay("vector", eng)

        @block.scalar
        def _(eng):
            S.replay("scalar", eng)

        @block.gpsimd
        def _(eng):
            S.replay("gpsimd", eng)

    return nc


_NC_CACHE = None


def kernel(L0_attn_u, L0_attn_Lambda_ri, L0_attn_p_ri, L0_attn_q_ri,
           L0_attn_Vc_ri, L0_attn_Ct_ri, L0_attn_Bp, L0_attn_D,
           L0_attn_log_step, L0_ffn_U1, L0_ffn_V1, L0_ffn_U2, L0_ffn_V2,
           L0_ffn_b1, L0_ffn_b2):
    global _NC_CACHE
    if _NC_CACHE is None:
        _NC_CACHE = build_program()
    nc = _NC_CACHE

    x = np.ascontiguousarray(np.asarray(L0_attn_u, dtype=np.float32)
                             .reshape(16384, 4096))
    vc = np.asarray(L0_attn_Vc_ri, dtype=np.float32)
    common = {
        "lam": np.ascontiguousarray(np.asarray(L0_attn_Lambda_ri, np.float32)),
        "p_ri": np.ascontiguousarray(np.asarray(L0_attn_p_ri, np.float32)),
        "q_ri": np.ascontiguousarray(np.asarray(L0_attn_q_ri, np.float32)),
        "ct_ri": np.ascontiguousarray(np.asarray(L0_attn_Ct_ri, np.float32)
                                      .reshape(64, 2)),
        "vc_reT": np.ascontiguousarray(vc[:, :, 0].T),
        "vc_imT": np.ascontiguousarray(vc[:, :, 1].T),
        "bp": np.ascontiguousarray(np.asarray(L0_attn_Bp, np.float32)),
        "dd": np.asarray(L0_attn_D, np.float32).reshape(1, 1),
        "ls": np.asarray(L0_attn_log_step, np.float32).reshape(1, 1),
        "u1": np.ascontiguousarray(np.asarray(L0_ffn_U1, np.float32)),
        "v1": np.ascontiguousarray(np.asarray(L0_ffn_V1, np.float32)),
        "u2": np.ascontiguousarray(np.asarray(L0_ffn_U2, np.float32)),
        "v2": np.ascontiguousarray(np.asarray(L0_ffn_V2, np.float32)),
        "b1t": np.ascontiguousarray(np.asarray(L0_ffn_b1, np.float32)
                                    .reshape(2 * FP, 128).T),
        "b2t": np.ascontiguousarray(np.asarray(L0_ffn_b2, np.float32)
                                    .reshape(DT, 128).T),
    }
    in_maps = []
    for c in range(N_CORES):
        m = dict(common)
        m["x_t"] = np.ascontiguousarray(x[c * TOK:(c + 1) * TOK, :].T)
        in_maps.append(m)
    res = run_bass_kernel_spmd(nc, in_maps, core_ids=list(range(N_CORES)))
    out = np.empty((16384, 4096), np.float32)
    for c in range(N_CORES):
        out[c * TOK:(c + 1) * TOK, :] = res.results[c]["out_t"].T
    return out.reshape(2, 8192, 4096)


# version 09 (final)
# speedup vs baseline: 1.1575x; 1.1575x over previous
import numpy as np
import concourse.bass as bass
import concourse.mybir as mybir
from concourse.bass_utils import run_bass_kernel_spmd
from concourse.ap import AP

F32 = mybir.dt.float32
BF16 = mybir.dt.bfloat16
AF = mybir.ActivationFunctionType
ALU = mybir.AluOpType

N_CORES = 8
D = 4096
DT = 32
TOK = 2048
CHUNK = 512
NCH = TOK // CHUNK
RANK = 1024
INTER = 11008
FP = 86
L = 4096
EPS = 1e-5
DEBUG = False


def _host_consts():
    jj = np.arange(L, dtype=np.float64)
    ang = (np.float64(-2.0 * np.pi) * (jj / L)).astype(np.float32)
    om = np.cos(ang.astype(np.float64)) + 1j * np.sin(ang.astype(np.float64))
    w = (1.0 - om) / (1.0 + om)
    c = 2.0 / (1.0 + om)
    wflat = np.stack([w.real, w.imag]).astype(np.float32)
    cri = np.concatenate([c.real.reshape(64, 64), c.imag.reshape(64, 64)],
                         axis=1).astype(np.float32)
    k64 = np.arange(64, dtype=np.float64)
    cos64 = np.cos(2 * np.pi * np.outer(k64, k64) / 64.0)
    sin64 = np.sin(2 * np.pi * np.outer(k64, k64) / 64.0)
    w64a = np.concatenate([cos64, -sin64], axis=0).astype(np.float32)
    w64b = np.concatenate([sin64, cos64], axis=0).astype(np.float32)
    w64c = (np.concatenate([cos64, -sin64], axis=0) / L).astype(np.float32)
    twr = np.cos(2 * np.pi * np.outer(k64, k64) / L)
    twi = np.sin(2 * np.pi * np.outer(k64, k64) / L)
    tw64 = np.zeros((128, 128), np.float32)
    tw64[0:64, 0:64] = twr; tw64[64:128, 0:64] = twr
    tw64[0:64, 64:128] = twi; tw64[64:128, 64:128] = twi
    ones1 = np.ones((1, 128), np.float32)
    onesbf = np.ones((128, 1), np.float32)
    sc = np.tile(np.array([[EPS, np.log(2.0), -0.5, -1.0, 1.0 / D]],
                          np.float32), (128, 1))
    return dict(wflat=wflat, cri=cri, w64a=w64a, w64b=w64b, w64c=w64c,
                tw64=tw64, ones1=ones1, onesbf=onesbf, sc=sc)


class Ctr:
    def __init__(self, h):
        self.h = h
        self.n = 0

    def inc(self, k=1):
        self.n += k
        return self.n


class Sched:
    def __init__(self):
        self.items = []

    def add(self, engine, fn):
        self.items.append((engine, fn))

    def replay(self, engine_name, eng):
        for name, fn in self.items:
            if name == engine_name:
                fn(eng)


class Buf:
    def __init__(self, t, w):
        self.t = t
        self.w = w

    def __getitem__(self, key):
        par, rows, cols = key
        if isinstance(cols, slice):
            c0 = cols.start if cols.start is not None else 0
            c1 = cols.stop if cols.stop is not None else self.w
        else:
            raise TypeError(cols)
        return self.t[rows, par * self.w + c0:par * self.w + c1]


def build_program():
    nc = bass.Bass("TRN2", target_bir_lowering=False)
    hc = _host_consts()

    x_d = nc.dram_tensor("x_t", [D, TOK], F32, kind="ExternalInput").ap()
    out_d = nc.dram_tensor("out_t", [D, TOK], F32, kind="ExternalOutput").ap()
    lam_d = nc.dram_tensor("lam", [64, 2], F32, kind="ExternalInput").ap()
    p_d = nc.dram_tensor("p_ri", [64, 2], F32, kind="ExternalInput").ap()
    q_d = nc.dram_tensor("q_ri", [64, 2], F32, kind="ExternalInput").ap()
    ct_d = nc.dram_tensor("ct_ri", [64, 2], F32, kind="ExternalInput").ap()
    vcret_d = nc.dram_tensor("vc_reT", [64, 64], F32, kind="ExternalInput").ap()
    vcimt_d = nc.dram_tensor("vc_imT", [64, 64], F32, kind="ExternalInput").ap()
    bp_d = nc.dram_tensor("bp", [64, 1], F32, kind="ExternalInput").ap()
    dd_d = nc.dram_tensor("dd", [1, 1], F32, kind="ExternalInput").ap()
    ls_d = nc.dram_tensor("ls", [1, 1], F32, kind="ExternalInput").ap()
    u1_d = nc.dram_tensor("u1", [D, RANK], F32, kind="ExternalInput").ap()
    v1_d = nc.dram_tensor("v1", [RANK, 2 * INTER], F32, kind="ExternalInput").ap()
    u2_d = nc.dram_tensor("u2", [INTER, RANK], F32, kind="ExternalInput").ap()
    v2_d = nc.dram_tensor("v2", [RANK, D], F32, kind="ExternalInput").ap()
    b1t_d = nc.dram_tensor("b1t", [128, 2 * FP], F32, kind="ExternalInput").ap()
    b2t_d = nc.dram_tensor("b2t", [128, DT], F32, kind="ExternalInput").ap()

    kpad_d = nc.dram_tensor("kpad", [4224], F32, kind="Internal").ap()
    ksums_dram = nc.dram_tensor("ksums_dram", [8, 4096], F32, kind="Internal").ap()
    u1s_d = nc.dram_tensor("u1s", [D, RANK], BF16, kind="Internal").ap()
    v1s_d = nc.dram_tensor("v1s", [FP, 128, 2048], BF16, kind="Internal").ap()
    u2s_d = nc.dram_tensor("u2s", [INTER, RANK], BF16, kind="Internal").ap()
    v2s_d = nc.dram_tensor("v2s", [16, 128, 2048], BF16, kind="Internal").ap()
    acts_d = nc.dram_tensor("acts", [FP, 128, CHUNK], BF16, kind="Internal").ap()
    dbg_d = (nc.dram_tensor("dbg", [128, 8192], F32, kind="ExternalOutput").ap()
             if DEBUG else None)
    x1d_d = (nc.dram_tensor("x1dump", [128, DT * CHUNK], F32,
                            kind="ExternalOutput").ap() if DEBUG else None)

    wflat_d = nc.inline_tensor(hc["wflat"], "c_wflat").ap()
    cri_d = nc.inline_tensor(hc["cri"], "c_cri").ap()
    w64a_d = nc.inline_tensor(hc["w64a"], "c_w64a").ap()
    w64b_d = nc.inline_tensor(hc["w64b"], "c_w64b").ap()
    w64c_d = nc.inline_tensor(hc["w64c"], "c_w64c").ap()
    tw64_d = nc.inline_tensor(hc["tw64"], "c_tw64").ap()
    ones1_d = nc.inline_tensor(hc["ones1"], "c_ones1").ap()
    onesbf_d = nc.inline_tensor(hc["onesbf"], "c_onesbf").ap()
    sc_d = nc.inline_tensor(hc["sc"], "c_sc").ap()

    import contextlib
    with contextlib.ExitStack() as _st:
        xbuf = _st.enter_context(nc.sbuf_tensor("xbuf", [128, DT * CHUNK], F32))
        ubf = _st.enter_context(nc.sbuf_tensor("ubf", [128, DT * CHUNK], BF16))
        tall = _st.enter_context(nc.sbuf_tensor("tall", [128, 4096], BF16))
        psb = _st.enter_context(nc.sbuf_tensor("psb", [128, 8 * CHUNK], BF16))
        ztsb = _st.enter_context(nc.sbuf_tensor("ztsb", [128, 8 * CHUNK], BF16))
        u1b = Buf(_st.enter_context(nc.sbuf_tensor("u1b", [128, 2 * 1024], BF16)), 1024)
        v1b = Buf(_st.enter_context(nc.sbuf_tensor("v1b", [128, 3 * 2048], BF16)), 2048)
        u2b = Buf(_st.enter_context(nc.sbuf_tensor("u2b", [128, 3 * 1024], BF16)), 1024)
        v2b = Buf(_st.enter_context(nc.sbuf_tensor("v2b", [128, 2 * 2048], BF16)), 2048)
        actw = Buf(_st.enter_context(nc.sbuf_tensor("actw", [128, 3 * CHUNK], BF16)), CHUNK)
        actr = Buf(_st.enter_context(nc.sbuf_tensor("actr", [128, 3 * CHUNK], BF16)), CHUNK)
        sgb = Buf(_st.enter_context(nc.sbuf_tensor("sgb", [128, 2 * CHUNK], BF16)), CHUNK)
        xbft = Buf(_st.enter_context(nc.sbuf_tensor("xbft", [128, 2 * CHUNK], BF16)), CHUNK)
        sqt = Buf(_st.enter_context(nc.sbuf_tensor("sqt", [128, 2 * CHUNK], BF16)), CHUNK)
        ttmp = Buf(_st.enter_context(nc.sbuf_tensor("ttmp", [128, 1 * CHUNK], F32)), CHUNK)
        rtmp = Buf(_st.enter_context(nc.sbuf_tensor("rtmp", [128, 1 * CHUNK], F32)), CHUNK)
        mb_sb = _st.enter_context(nc.sbuf_tensor("mb_sb", [128, CHUNK], F32))
        sb_sb = _st.enter_context(nc.sbuf_tensor("sb_sb", [128, CHUNK], F32))
        lam_s = _st.enter_context(nc.sbuf_tensor("lam_s", [64, 2], F32))
        p_s = _st.enter_context(nc.sbuf_tensor("p_s", [64, 2], F32))
        q_s = _st.enter_context(nc.sbuf_tensor("q_s", [64, 2], F32))
        ct_s = _st.enter_context(nc.sbuf_tensor("ct_s", [64, 2], F32))
        vcret_s = _st.enter_context(nc.sbuf_tensor("vcret_s", [64, 64], F32))
        vcimt_s = _st.enter_context(nc.sbuf_tensor("vcimt_s", [64, 64], F32))
        bp_s = _st.enter_context(nc.sbuf_tensor("bp_s", [64, 1], F32))
        dd_s = _st.enter_context(nc.sbuf_tensor("dd_s", [1, 1], F32))
        ls_s = _st.enter_context(nc.sbuf_tensor("ls_s", [1, 1], F32))
        dsb = _st.enter_context(nc.sbuf_tensor("dsb", [128, 1], F32))
        b1t_s = _st.enter_context(nc.sbuf_tensor("b1t_s", [128, 2 * FP], F32))
        b2t_s = _st.enter_context(nc.sbuf_tensor("b2t_s", [128, DT], F32))
        t2s_s = _st.enter_context(nc.sbuf_tensor("t2s_s", [1, 1], F32))
        cri_s = _st.enter_context(nc.sbuf_tensor("cri_s", [64, 128], F32))
        w64a_s = _st.enter_context(nc.sbuf_tensor("w64a_s", [128, 64], F32))
        w64b_s = _st.enter_context(nc.sbuf_tensor("w64b_s", [128, 64], F32))
        w64c_s = _st.enter_context(nc.sbuf_tensor("w64c_s", [128, 64], F32))
        tw64_s = _st.enter_context(nc.sbuf_tensor("tw64_s", [128, 128], F32))
        ones1_s = _st.enter_context(nc.sbuf_tensor("ones1_s", [1, 128], F32))
        onesbf_s = _st.enter_context(nc.sbuf_tensor("onesbf_s", [128, 1], BF16))
        sc_s = _st.enter_context(nc.sbuf_tensor("sc_s", [128, 5], F32))
        w8 = _st.enter_context(nc.sbuf_tensor("w8", [128, 8], F32))
        wtmp = _st.enter_context(nc.sbuf_tensor("wtmp", [64, 8], F32))
        bri = _st.enter_context(nc.sbuf_tensor("bri", [64, 2], F32))
        invri = Buf(_st.enter_context(nc.sbuf_tensor("invri", [128, 1 * 512], F32)), 512)
        denre = _st.enter_context(nc.sbuf_tensor("denre", [64, 512], F32))
        denim = _st.enter_context(nc.sbuf_tensor("denim", [64, 512], F32))
        sq1 = _st.enter_context(nc.sbuf_tensor("sq1", [64, 512], F32))
        sq2 = _st.enter_context(nc.sbuf_tensor("sq2", [64, 512], F32))
        ksc = _st.enter_context(nc.sbuf_tensor("ksc", [8, 512], F32))
        wrec = _st.enter_context(nc.sbuf_tensor("wrec", [1, 512], F32))
        wimc = _st.enter_context(nc.sbuf_tensor("wimc", [1, 512], F32))
        kst = _st.enter_context(nc.sbuf_tensor("kst", [64, 512], F32))
        arri = _st.enter_context(nc.sbuf_tensor("arri", [128, 64], F32))
        atm = _st.enter_context(nc.sbuf_tensor("atm", [64, 384], F32))
        bri2 = _st.enter_context(nc.sbuf_tensor("bri2", [128, 64], F32))
        twt1 = _st.enter_context(nc.sbuf_tensor("twt1", [128, 64], F32))
        twt2 = _st.enter_context(nc.sbuf_tensor("twt2", [128, 64], F32))
        ksb = _st.enter_context(nc.sbuf_tensor("ksb", [64, 64], F32))
        zrow = _st.enter_context(nc.sbuf_tensor("zrow", [1, 128], F32))
        mrow = _st.enter_context(nc.sbuf_tensor("mrow", [1, CHUNK], F32))
        msrow = _st.enter_context(nc.sbuf_tensor("msrow", [1, CHUNK], F32))
        varrow = _st.enter_context(nc.sbuf_tensor("varrow", [1, CHUNK], F32))
        lnrow = _st.enter_context(nc.sbuf_tensor("lnrow", [1, CHUNK], F32))
        srow = _st.enter_context(nc.sbuf_tensor("srow", [1, CHUNK], F32))
        pb0 = _st.enter_context(nc.psum_tensor("pb0", [128, 512], F32))
        pb1 = _st.enter_context(nc.psum_tensor("pb1", [128, 512], F32))
        pb2 = _st.enter_context(nc.psum_tensor("pb2", [128, 512], F32))
        pb3 = _st.enter_context(nc.psum_tensor("pb3", [128, 512], F32))
        pb4 = _st.enter_context(nc.psum_tensor("pb4", [128, 512], F32))
        pb5 = _st.enter_context(nc.psum_tensor("pb5", [128, 512], F32))
        pb6 = _st.enter_context(nc.psum_tensor("pb6", [128, 512], F32))
        pb7 = _st.enter_context(nc.psum_tensor("pb7", [128, 512], F32))
        h_sync = _st.enter_context(nc.semaphore("s_sync"))
        h_gpd = _st.enter_context(nc.semaphore("s_gpd"))
        _chans = {}
        def mkchan(name, serialize, depth=1):
            h = _st.enter_context(nc.semaphore(name))
            c = Ctr(h)
            c.serialize = serialize
            c.depth = depth
            _chans[name] = c
            return c
        ch_small = mkchan("c_small", False)
        ch_cu1 = mkchan("c_cu1", False)
        ch_cv1 = [mkchan(f"c_cv1_{i}", True, depth=3) for i in range(4)]
        ch_cu2 = [mkchan(f"c_cu2_{i}", True, depth=3) for i in range(2)]
        ch_cv2 = [mkchan(f"c_cv2_{i}", True, depth=3) for i in range(2)]
        ch_kp = mkchan("c_kp", True)
        ch_tall = mkchan("c_tall", False)
        ch_x = [mkchan(f"c_x{q}", True) for q in range(4)]
        ch_u1b = [mkchan(f"c_u1b{i}", True) for i in range(2)]
        ch_v1b = [mkchan(f"c_v1b{i}", True) for i in range(3)]
        ch_u2b = [mkchan(f"c_u2b{i}", True) for i in range(3)]
        ch_actr = [mkchan(f"c_actr{i}", True) for i in range(3)]
        ch_v2b = [mkchan(f"c_v2b{i}", True) for i in range(2)]
        ch_actw = [mkchan(f"c_actw{i}", True) for i in range(3)]
        ch_out = [mkchan(f"c_out{q}", True) for q in range(4)]
        ch_dbg = mkchan("c_dbg", True)
        h_pe = _st.enter_context(nc.semaphore("s_pe"))
        h_dve = _st.enter_context(nc.semaphore("s_dve"))
        h_sc = _st.enter_context(nc.semaphore("s_sc"))
        h_gp = _st.enter_context(nc.semaphore("s_gp"))
        block = _st.enter_context(nc.Block())

        pb = [pb0, pb1, pb2, pb3, pb4, pb5, pb6, pb7]
        S = Sched()
        c_sync = Ctr(h_sync)
        c_gpd = Ctr(h_gpd)
        c_pe = Ctr(h_pe)
        c_dve = Ctr(h_dve)
        c_sc = Ctr(h_sc)
        c_gp = Ctr(h_gp)

        def dma(qname, chan, out_ap, in_ap, wait=None):
            idx = chan.n // 16
            tgt = chan.inc(16)
            ser = getattr(chan, "serialize", False)
            depth = getattr(chan, "depth", 1)
            def f(eng, out_ap=out_ap, in_ap=in_ap, wait=wait, idx=idx):
                if ser and idx >= depth:
                    eng.wait_ge(chan.h, 16 * (idx - depth + 1))
                if wait:
                    for (h, v) in wait:
                        eng.wait_ge(h, v)
                eng.dma_start(out=out_ap, in_=in_ap).then_inc(chan.h, 16)
            S.add(qname, f)
            return (chan.h, tgt)

        def ins(ename, ctr, fn, wait=None):
            tgt = ctr.inc(1)
            def f(eng, fn=fn, wait=wait):
                if wait:
                    for (h, v) in wait:
                        eng.wait_ge(h, v)
                fn(eng).then_inc(ctr.h, 1)
            S.add(ename, f)
            return tgt

        def dbg(col0, src_ap, wait):
            if not DEBUG:
                return
            rows, cols = src_ap.shape[0], src_ap.shape[1]
            dma("gpsimd", c_gpd,
                AP(dbg_d.tensor, col0, [[4096, rows], [1, cols]]), src_ap,
                wait=wait)

        def mm_group(mms, wait=None, skip_check=False):
            tgt = c_pe.inc(1)
            def f(eng, mms=mms, wait=wait):
                if wait:
                    for (h, v) in wait:
                        eng.wait_ge(h, v)
                last = None
                for (o, lf, rf, st, sp) in mms:
                    last = eng.matmul(o, lf(), rf(), start=st, stop=sp,
                                      skip_group_check=skip_check)
                last.then_inc(h_pe, 1)
            S.add("tensor", f)
            return tgt

        g_small = None
        for dst, src in [
            (lam_s[:], lam_d), (p_s[:], p_d), (q_s[:], q_d), (ct_s[:], ct_d),
            (vcret_s[:], vcret_d), (vcimt_s[:], vcimt_d), (bp_s[:], bp_d),
            (dd_s[:], dd_d), (ls_s[:], ls_d),
            (b1t_s[:], b1t_d), (b2t_s[:], b2t_d),
            (cri_s[:], cri_d),
            (w64a_s[:], w64a_d), (w64b_s[:], w64b_d), (w64c_s[:], w64c_d),
            (tw64_s[:], tw64_d), (ones1_s[:], ones1_d), (onesbf_s[:], onesbf_d),
            (sc_s[:], sc_d),
        ]:
            g_small = dma("gpsimd", ch_small, dst, src)

        t_sc1 = ins("scalar", c_sc, lambda e: e.activation(
            t2s_s[0:1, 0:1], ls_s[:], AF.Exp,
            scale=sc_s[0:1, 3:4], bias=sc_s[0:1, 1:2]),
            wait=[g_small])
        t_bre = mm_group([(pb7[0:64, 0:1], lambda: vcret_s[:],
                           lambda: bp_s[:], True, True)],
                         wait=[g_small])
        t_bim = mm_group([(pb7[0:64, 1:2], lambda: vcimt_s[:],
                           lambda: bp_s[:], True, True)], skip_check=True)
        t_evb = ins("scalar", c_sc, lambda e: e.copy(bri[:], pb7[0:64, 0:2]),
                    wait=[(h_pe, t_bim)])
        t_bd = mm_group([(pb5[:, 0:1], lambda: ones1_s[0:1, :],
                          lambda: dd_s[:], True, True)])
        t_evd = ins("scalar", c_sc, lambda e: e.copy(dsb[:], pb5[:, 0:1]),
                    wait=[(h_pe, t_bd)])

        wpairs = [(ct_s, bri), (ct_s, p_s), (q_s, bri), (q_s, p_s)]

        def wprods(e):
            last = None
            for k, (a, b) in enumerate(wpairs):
                last = e.tensor_tensor(sq1[0:64, 4 * k:4 * k + 1],
                                       a[:, 0:1], b[:, 0:1], ALU.mult)
                last = e.tensor_tensor(sq1[0:64, 4 * k + 1:4 * k + 2],
                                       a[:, 1:2], b[:, 1:2], ALU.mult)
                last = e.tensor_tensor(sq1[0:64, 4 * k + 2:4 * k + 3],
                                       a[:, 0:1], b[:, 1:2], ALU.mult)
                last = e.tensor_tensor(sq1[0:64, 4 * k + 3:4 * k + 4],
                                       a[:, 1:2], b[:, 0:1], ALU.mult)
            return last
        t_wp = ins("vector", c_dve, wprods, wait=[(h_sc, t_evb)])

        def wcombine(e):
            last = None
            for k in range(4):
                last = e.tensor_tensor(wtmp[:, 2 * k:2 * k + 1],
                                       sq1[0:64, 4 * k:4 * k + 1],
                                       sq1[0:64, 4 * k + 1:4 * k + 2], ALU.add)
                last = e.tensor_tensor(wtmp[:, 2 * k + 1:2 * k + 2],
                                       sq1[0:64, 4 * k + 2:4 * k + 3],
                                       sq1[0:64, 4 * k + 3:4 * k + 4],
                                       ALU.subtract)
            return last
        t_w = ins("vector", c_dve, wcombine, wait=[(h_dve, t_wp)])

        def build_w8(e):
            last = None
            for k in range(4):
                wr = wtmp[:, 2 * k:2 * k + 1]
                wi = wtmp[:, 2 * k + 1:2 * k + 2]
                e.tensor_copy(w8[0:64, 2 * k:2 * k + 1], wr)
                e.tensor_scalar(w8[64:128, 2 * k:2 * k + 1], wi, -1.0, None,
                                ALU.mult)
                e.tensor_copy(w8[0:64, 2 * k + 1:2 * k + 2], wi)
                last = e.tensor_copy(w8[64:128, 2 * k + 1:2 * k + 2], wr)
            return last
        t_w8 = ins("vector", c_dve, build_w8, wait=[(h_dve, t_w)])

        prev_ev = 0
        prev_dve = 0
        prev_kdma = 0
        prev_cau = 0
        prev_g2 = 0
        for jc in range(8):
            sl = slice(jc * 512, (jc + 1) * 512)
            par = jc % 2
            t_wre = dma("gpsimd", ch_kp, wrec[:], wflat_d[0:1, sl])
            t_wim = dma("gpsimd", ch_kp, wimc[:], wflat_d[1:2, sl])
            wgc = [t_wim, (h_sc, t_sc1)]
            if jc >= 1:
                wgc.append((h_pe, prev_g2))
            ins("vector", c_dve, lambda e: e.tensor_scalar(
                wrec[:], wrec[:], t2s_s[0:1, 0:1], None, ALU.mult), wait=wgc)
            t_gch = ins("vector", c_dve, lambda e: e.tensor_scalar(
                wimc[:], wimc[:], t2s_s[0:1, 0:1], None, ALU.mult))
            wg = [(h_dve, t_gch)] if jc == 0 else [(h_dve, prev_dve),
                                                   (h_dve, t_gch)]
            t_g1 = mm_group([(pb0[0:64, :], lambda: ones1_s[0:1, 0:64],
                              lambda: wrec[:], True, True)],
                            wait=wg, skip_check=True)
            t_g2 = mm_group([(pb1[0:64, :], lambda: ones1_s[0:1, 0:64],
                              lambda: wimc[:], True, True)],
                            skip_check=True)
            prev_g2 = t_g2
            ins("vector", c_dve, lambda e: e.tensor_scalar(
                denre[:], pb0[0:64, :], lam_s[:, 0:1], None, ALU.subtract),
                wait=[(h_pe, t_g2)])
            ins("vector", c_dve, lambda e: e.tensor_scalar(
                denim[:], pb1[0:64, :], lam_s[:, 1:2], None, ALU.subtract))
            ins("vector", c_dve, lambda e: e.tensor_tensor(
                sq1[:], denre[:], denre[:], ALU.mult))
            ins("vector", c_dve, lambda e: e.tensor_tensor(
                sq2[:], denim[:], denim[:], ALU.mult))
            t_mag = ins("vector", c_dve, lambda e: e.tensor_tensor(
                kst[:], sq1[:], sq2[:], ALU.add))
            t_rec = ins("vector", c_dve, lambda e: e.reciprocal(
                sq2[:], kst[:]), wait=[(h_dve, t_mag)])
            wir = [(h_dve, t_rec)]
            if jc >= 1:
                wir.append((h_pe, prev_cau))
            ins("vector", c_dve, lambda e: e.tensor_tensor(
                invri[0, 0:64, :], denre[:], sq2[:], ALU.mult), wait=wir)
            t_ii = ins("vector", c_dve, lambda e: e.scalar_tensor_tensor(
                invri[0, 64:128, :], denim[:], -1.0, sq2[:], ALU.mult,
                ALU.mult))
            prev_dve = t_ii
            wc = [(h_dve, t_ii)]
            if jc == 0:
                wc.append((h_dve, t_w8))
            else:
                wc.append((h_sc, prev_ev))
            t_cau = mm_group([(pb2[0:8, :], lambda: w8[:],
                               lambda: invri[0, :, :], True, True)],
                             wait=wc, skip_check=True)
            prev_cau = t_cau
            wev = [(h_pe, t_cau)]
            if jc >= 1:
                wev.append(prev_kdma)
            prev_ev = ins("scalar", c_sc, lambda e: e.copy(
                ksc[:], pb2[0:8, :]), wait=wev)
            prev_kdma = dma("gpsimd", ch_kp,
                            AP(ksums_dram.tensor, jc * 512, [[4096, 8], [1, 512]]),
                            ksc[:], wait=[(h_sc, prev_ev)])
        t_kdram = prev_kdma
        t_kst = None
        for r in range(8):
            t_kst = dma("gpsimd", ch_kp, kst[:, r * 64:(r + 1) * 64],
                        AP(ksums_dram.tensor, r * 4096, [[64, 64], [1, 64]]),
                        wait=[t_kdram] if r == 0 else None)

        KB = lambda r: kst[:, r * 64:(r + 1) * 64]
        A0, A1, A2, A3, A4, A5 = (atm[:, i * 64:(i + 1) * 64] for i in range(6))

        def at_roots(e):
            e.tensor_scalar(A0[:], KB(6), 1.0, None, ALU.add)
            e.tensor_tensor(A1[:], KB(2), KB(4), ALU.mult)
            e.tensor_tensor(A2[:], KB(3), KB(5), ALU.mult)
            e.tensor_tensor(A1[:], A1[:], A2[:], ALU.subtract)
            e.tensor_tensor(A2[:], KB(2), KB(5), ALU.mult)
            e.tensor_tensor(A3[:], KB(3), KB(4), ALU.mult)
            e.tensor_tensor(A2[:], A2[:], A3[:], ALU.add)
            e.tensor_tensor(A3[:], A0[:], A0[:], ALU.mult)
            e.tensor_tensor(A4[:], KB(7), KB(7), ALU.mult)
            e.tensor_tensor(A3[:], A3[:], A4[:], ALU.add)
            e.reciprocal(A3[:], A3[:])
            e.tensor_tensor(A4[:], A1[:], A0[:], ALU.mult)
            e.tensor_tensor(A5[:], A2[:], KB(7), ALU.mult)
            e.tensor_tensor(A4[:], A4[:], A5[:], ALU.add)
            e.tensor_tensor(A4[:], A4[:], A3[:], ALU.mult)
            e.tensor_tensor(A5[:], A2[:], A0[:], ALU.mult)
            e.tensor_tensor(A0[:], A1[:], KB(7), ALU.mult)
            e.tensor_tensor(A5[:], A5[:], A0[:], ALU.subtract)
            e.tensor_tensor(A5[:], A5[:], A3[:], ALU.mult)
            e.tensor_tensor(A4[:], KB(0), A4[:], ALU.subtract)
            e.tensor_tensor(A5[:], KB(1), A5[:], ALU.subtract)
            e.tensor_tensor(A0[:], cri_s[:, 0:64], A4[:], ALU.mult)
            e.tensor_tensor(A1[:], cri_s[:, 64:128], A5[:], ALU.mult)
            e.tensor_tensor(arri[0:64, :], A0[:], A1[:], ALU.subtract)
            e.tensor_tensor(A0[:], cri_s[:, 0:64], A5[:], ALU.mult)
            e.tensor_tensor(A1[:], cri_s[:, 64:128], A4[:], ALU.mult)
            return e.tensor_tensor(arri[64:128, :], A0[:], A1[:], ALU.add)
        t_ar = ins("vector", c_dve, at_roots, wait=[t_kst])

        t_fa = mm_group([(pb3[0:64, 0:64], lambda: arri[:],
                          lambda: w64a_s[:], True, True)],
                        wait=[(h_dve, t_ar)], skip_check=True)
        t_fb = mm_group([(pb4[0:64, 0:64], lambda: arri[:],
                          lambda: w64b_s[:], True, True)], skip_check=True)

        def mk_hi(o, rf):
            def f(eng):
                return eng.matmul(o, arri[:], rf(), start=True, stop=True,
                                  tile_position=(0, 64), skip_group_check=True)
            return f
        t_fah = ins("tensor", c_pe, mk_hi(pb3[64:128, 0:64], lambda: w64a_s[:]))
        t_fbh = ins("tensor", c_pe, mk_hi(pb4[64:128, 0:64], lambda: w64b_s[:]))

        def twiddle(e):
            e.tensor_tensor(twt1[0:64, :], pb3[0:64, 0:64],
                            tw64_s[0:64, 0:64], ALU.mult)
            e.tensor_tensor(twt2[0:64, :], pb4[0:64, 0:64],
                            tw64_s[0:64, 64:128], ALU.mult)
            e.tensor_tensor(bri2[0:64, :], twt1[0:64, :], twt2[0:64, :],
                            ALU.subtract)
            e.tensor_tensor(twt1[64:128, :], pb3[64:128, 0:64],
                            tw64_s[64:128, 64:128], ALU.mult)
            e.tensor_tensor(twt2[64:128, :], pb4[64:128, 0:64],
                            tw64_s[64:128, 0:64], ALU.mult)
            return e.tensor_tensor(bri2[64:128, :], twt1[64:128, :],
                                   twt2[64:128, :], ALU.add)
        t_tw = ins("vector", c_dve, twiddle, wait=[(h_pe, t_fbh)])
        t_fk = mm_group([(pb3[0:64, 64:128], lambda: w64c_s[:],
                          lambda: bri2[:], True, True)],
                        wait=[(h_dve, t_tw)], skip_check=True)
        t_evk = ins("scalar", c_sc, lambda e: e.copy(ksb[:], pb3[0:64, 64:128]),
                    wait=[(h_pe, t_fk)])
        t_z = ins("vector", c_dve, lambda e: e.memset(zrow[:], 0.0))
        t_zd = dma("gpsimd", ch_kp, AP(kpad_d.tensor, 0, [[1, 1], [1, 127]]),
                   zrow[0:1, 0:127], wait=[(h_dve, t_z)])
        t_kd = dma("gpsimd", ch_kp, AP(kpad_d.tensor, 127, [[64, 64], [1, 64]]),
                   ksb[:], wait=[(h_sc, t_evk)])
        t_kd2 = dma("gpsimd", ch_kp, AP(kpad_d.tensor, 4223, [[1, 1], [1, 1]]),
                    zrow[0:1, 0:1])
        for tau in range(128):
            dma("gpsimd", ch_tall, tall[tau:tau + 1, :],
                         AP(kpad_d.tensor, 127 - tau, [[1, 1], [1, 4096]]),
                         wait=[t_kd2] if tau == 0 else None)

        u1_cast_tgt = None
        for i in range(4):
            u1_cast_tgt = dma("gpsimd", ch_cu1,
                              u1s_d[i * 1024:(i + 1) * 1024, :],
                              u1_d[i * 1024:(i + 1) * 1024, :])
        v1_cast_tgt = []
        for f in range(FP):
            src_g = AP(v1_d.tensor, f * 128,
                       [[2 * INTER, 128], [128 * 2 * INTER, 8], [1, 128]])
            dma("gpsimd", ch_cv1[f % 4], v1s_d[f, :, 0:1024], src_g)
            src_u = AP(v1_d.tensor, INTER + f * 128,
                       [[2 * INTER, 128], [128 * 2 * INTER, 8], [1, 128]])
            v1_cast_tgt.append(dma("gpsimd", ch_cv1[f % 4],
                                   v1s_d[f, :, 1024:2048], src_u))
        u2_cast_tgt = []
        for f in range(FP):
            r0 = f * 128
            u2_cast_tgt.append(dma("gpsimd", ch_cu2[f % 2],
                                   u2s_d[r0:r0 + 128, :],
                                   u2_d[r0:r0 + 128, :]))
        v2_cast_tgt = []
        for mg in range(16):
            src = AP(v2_d.tensor, mg * 256, [[D, 128], [128 * D, 8], [1, 256]])
            v2_cast_tgt.append(dma("gpsimd", ch_cv2[mg % 2], v2s_d[mg, :, :], src))


        t_tall = (ch_tall.h, ch_tall.n)

        def overshoot(tgts, nch, depth, chans):
            out = []
            for i in range(len(tgts)):
                j = i + nch * depth
                if j < len(tgts):
                    out.append(tgts[j])
                else:
                    c = chans[i % nch]
                    out.append((c.h, c.n))
            return out
        v1_cast_w = overshoot(v1_cast_tgt, 4, 3, ch_cv1)
        u2_cast_w = overshoot(u2_cast_tgt, 2, 3, ch_cu2)
        v2_cast_w = overshoot(v2_cast_tgt, 2, 3, ch_cv2)
        cast_u1_done = (ch_cu1.h, ch_cu1.n)

        XB = lambda dt: xbuf[:, dt * CHUNK:(dt + 1) * CHUNK]
        UB = lambda dt: ubf[:, dt * CHUNK:(dt + 1) * CHUNK]

        prev_out = None
        prev_actr = None
        for ch in range(NCH):
            tok0 = ch * CHUNK
            first = (ch == 0)

            x_tgt = []
            for qd in range(4):
                src = AP(x_d.tensor, (qd * 8 * 128) * TOK + tok0,
                         [[TOK, 128], [128 * TOK, 8], [1, CHUNK]])
                dst = xbuf[:, qd * 8 * CHUNK:(qd + 1) * 8 * CHUNK] \
                    .rearrange("p (dt c) -> p dt c", dt=8)
                wl = [prev_out[qd]] if prev_out is not None else None
                x_tgt.append(dma("sync", ch_x[qd], dst, src, wait=wl))

            def emit_ln(wait_dt_ready, extra_first_wait):
                stat_tgts = [None] * DT
                for dt in range(DT):
                    par = dt % 2
                    w = list(wait_dt_ready(dt))
                    if dt >= 2:
                        w.append((h_pe, stat_tgts[dt - 2]))
                    t1 = ins("scalar", c_sc, lambda e, dt=dt, par=par: e.copy(
                        xbft[par, :, :], XB(dt)), wait=w)
                    t2 = ins("vector", c_dve, lambda e, dt=dt, par=par:
                             e.tensor_tensor(sqt[par, :, :], XB(dt), XB(dt),
                                             ALU.mult), wait=w)
                    wmm = [(h_sc, t1), (h_dve, t2)]
                    if dt == 0 and extra_first_wait:
                        wmm += extra_first_wait
                    stat_tgts[dt] = mm_group([
                        (pb6[0:1, :], lambda: onesbf_s[:],
                         lambda par=par: xbft[par, :, :], dt == 0, dt == DT - 1),
                        (pb7[0:1, :], lambda: onesbf_s[:],
                         lambda par=par: sqt[par, :, :], dt == 0, dt == DT - 1),
                    ], wait=wmm, skip_check=True)
                t_mean = ins("vector", c_dve, lambda e: e.tensor_scalar(
                    mrow[:], pb6[0:1, :], 1.0 / D, None, ALU.mult),
                    wait=[(h_pe, stat_tgts[DT - 1])])
                ins("vector", c_dve, lambda e: e.tensor_tensor(
                    msrow[:], mrow[:], mrow[:], ALU.mult))
                t_var = ins("vector", c_dve, lambda e: e.scalar_tensor_tensor(
                    varrow[:], pb7[0:1, :], 1.0 / D, msrow[:], ALU.mult,
                    ALU.subtract))
                ins("scalar", c_sc, lambda e: e.activation(
                    lnrow[:], varrow[:], AF.Ln, bias=sc_s[0:1, 0:1]),
                    wait=[(h_dve, t_var)])
                t_rstd = ins("scalar", c_sc, lambda e: e.activation(
                    srow[:], lnrow[:], AF.Exp, scale=sc_s[0:1, 2:3]))
                t_mb = mm_group([(pb6[:, :], lambda: ones1_s[:],
                                  lambda: mrow[:], True, True)],
                                wait=[(h_dve, t_mean), (h_dve, t_var)],
                                skip_check=True)
                t_sb = mm_group([(pb7[:, :], lambda: ones1_s[:],
                                  lambda: srow[:], True, True)],
                                wait=[(h_sc, t_rstd)], skip_check=True)
                ev_m = ins("scalar", c_sc, lambda e: e.copy(mb_sb[:], pb6[:, :]),
                           wait=[(h_pe, t_mb)])
                ev_s = ins("scalar", c_sc, lambda e: e.copy(sb_sb[:], pb7[:, :]),
                           wait=[(h_pe, t_sb)])
                u_tgts = [None] * DT
                for dt in range(DT):
                    w = [(h_sc, ev_s)]
                    ins("vector", c_dve, lambda e, dt=dt: e.tensor_tensor(
                        ttmp[0, :, :], XB(dt), mb_sb[:], ALU.subtract),
                        wait=w)
                    u_tgts[dt] = ins("vector", c_dve, lambda e, dt=dt:
                                     e.tensor_tensor(UB(dt), ttmp[0, :, :],
                                                     sb_sb[:], ALU.mult))
                return u_tgts, ev_s

            u_tgts, ln1_evs = emit_ln(lambda dt: [x_tgt[dt // 8]],
                                      [g_small] if first else None)

            conv_tgts = [None] * DT
            ep_v3 = [None] * DT
            ep_v4 = [None] * DT

            def emit_conv(i):
                bank = pb[i % 4]
                w = [(h_dve, u_tgts[i])]
                if first and i == 0:
                    w.append(t_tall)
                if i >= 4:
                    w.append((h_dve, ep_v3[i - 4]))
                mms = []
                for jj in range(i + 1):
                    d = i - jj
                    mms.append((bank[:, :],
                                (lambda d=d: tall[:, d * 128:(d + 1) * 128]),
                                (lambda jj=jj: UB(jj)),
                                jj == 0, jj == i))
                conv_tgts[i] = mm_group(mms, wait=w, skip_check=True)

            def emit_ep(i):
                wv = [(h_pe, conv_tgts[i])]
                if first and i == 0:
                    wv.append((h_sc, t_evd))
                ep_v3[i] = ins("vector", c_dve, lambda e, i=i:
                               e.tensor_tensor(rtmp[0, :, :], pb[i % 4][:, :],
                                               XB(i), ALU.add), wait=wv)
                ep_v4[i] = ins("vector", c_dve, lambda e, i=i:
                               e.scalar_tensor_tensor(XB(i), UB(i), dsb[:, 0:1],
                                                      rtmp[0, :, :], ALU.mult,
                                                      ALU.add))

            for i in range(DT):
                if i >= 4:
                    emit_ep(i - 4)
                emit_conv(i)
            for i in range(DT - 4, DT):
                emit_ep(i)

            if DEBUG and ch == 0:
                dma("sync", ch_dbg, x1d_d, xbuf[:, :],
                    wait=[(h_dve, ep_v4[DT - 1])])

            v_tgts, ln2_evs = emit_ln(lambda dt: [(h_dve, ep_v4[dt])], None)

            p_mm = [None] * DT
            u1_tgt = [None] * DT
            u1_tgt[0] = dma("sync", ch_u1b[0], u1b[0, :, :], u1s_d[0:128, :],
                            wait=[cast_u1_done] if first else None)
            u1_tgt[1] = dma("sync", ch_u1b[1], u1b[1, :, :], u1s_d[128:256, :],
                            wait=[cast_u1_done] if first else None)
            for k in range(DT):
                par = k % 2
                w = [u1_tgt[k], (h_dve, v_tgts[k])]
                if k == 0:
                    w.append((h_sc, ln2_evs))
                mms = []
                for m in range(8):
                    mms.append((pb[m][:, :],
                                (lambda par=par, m=m:
                                 u1b[par, :, m * 128:(m + 1) * 128]),
                                (lambda k=k: UB(k)),
                                k == 0, k == DT - 1))
                p_mm[k] = mm_group(mms, wait=w, skip_check=True)
                if k + 2 < DT:
                    u1_tgt[k + 2] = dma("sync", ch_u1b[(k + 2) % 2],
                                        u1b[(k + 2) % 2, :, :],
                                        u1s_d[(k + 2) * 128:(k + 3) * 128, :],
                                        wait=[(h_pe, p_mm[k])])
            p_ev = []
            for m in range(8):
                p_ev.append(ins("scalar", c_sc, lambda e, m=m: e.copy(
                    psb[:, m * CHUNK:(m + 1) * CHUNK], pb[m][:, :]),
                    wait=[(h_pe, p_mm[DT - 1])] if m == 0 else None))
            t_pev = p_ev[-1]
            u2_tgt = [None] * FP
            actr_tgt = [None] * FP
            a_sc = [None] * FP
            a_dve = [None] * FP
            a_actdma = [None] * FP
            a_mm_g = [None] * FP
            a_mm_u = [None] * FP
            v1_tgt = [None] * FP
            for f0 in range(3):
                v1_tgt[f0] = dma("sync", ch_v1b[f0], v1b[f0, :, :],
                                 v1s_d[f0, :, :],
                                 wait=[v1_cast_w[f0]] if first else None)
            for f in range(FP):
                par3 = f % 3
                par2 = f % 2
                par4 = f % 3
                gbank = pb[(f % 4) * 2]
                ubank = pb[(f % 4) * 2 + 1]
                w = [v1_tgt[f]]
                if f == 0:
                    w.append((h_sc, t_pev))
                if f >= 4:
                    w.append((h_sc, a_sc[f - 4]))
                    w.append((h_dve, a_dve[f - 4]))
                mmsg = []
                for k2 in range(8):
                    mmsg.append((gbank[:, :],
                                 (lambda par3=par3, k2=k2:
                                  v1b[par3, :, k2 * 128:(k2 + 1) * 128]),
                                 (lambda k2=k2:
                                  psb[:, k2 * CHUNK:(k2 + 1) * CHUNK]),
                                 k2 == 0, k2 == 7))
                a_mm_g[f] = mm_group(mmsg, wait=w, skip_check=True)
                mmsu = []
                for k2 in range(8):
                    mmsu.append((ubank[:, :],
                                 (lambda par3=par3, k2=k2:
                                  v1b[par3, :, 1024 + k2 * 128:1024 + (k2 + 1) * 128]),
                                 (lambda k2=k2:
                                  psb[:, k2 * CHUNK:(k2 + 1) * CHUNK]),
                                 k2 == 0, k2 == 7))
                a_mm_u[f] = mm_group(mmsu, skip_check=True)
                if f + 3 < FP and not first:
                    v1_tgt[f + 3] = dma("sync", ch_v1b[(f + 3) % 3],
                                        v1b[(f + 3) % 3, :, :],
                                        v1s_d[f + 3, :, :],
                                        wait=[(h_pe, a_mm_u[f])])
                wsc = [(h_pe, a_mm_g[f])]
                if f >= 2:
                    wsc.append((h_dve, a_dve[f - 2]))
                a_sc[f] = ins("scalar", c_sc, lambda e, f=f, par2=par2,
                              gbank=gbank: e.activation(
                    sgb[par2, :, :], gbank[:, :], AF.Silu,
                    bias=b1t_s[:, f:f + 1]), wait=wsc)
                wdv = [(h_pe, a_mm_u[f]), (h_sc, a_sc[f])]
                if f >= 3:
                    wdv.append(a_actdma[f - 3])
                a_dve[f] = ins("vector", c_dve, lambda e, f=f, par2=par2,
                               par4=par4, ubank=ubank: e.scalar_tensor_tensor(
                    actw[par4, :, :], ubank[:, :],
                    b1t_s[:, FP + f:FP + f + 1], sgb[par2, :, :],
                    ALU.add, ALU.mult), wait=wdv)
                wact = [(h_dve, a_dve[f])]
                if prev_actr is not None:
                    wact.append(prev_actr[f])
                a_actdma[f] = dma("sync", ch_actw[par4], acts_d[f, :, :],
                                  actw[par4, :, :], wait=wact)
                if f + 3 < FP and first:
                    v1_tgt[f + 3] = dma("sync", ch_v1b[(f + 3) % 3],
                                        v1b[(f + 3) % 3, :, :],
                                        v1s_d[f + 3, :, :],
                                        wait=[(h_pe, a_mm_u[f]),
                                              v1_cast_w[f + 3]])
                if f == 50:
                    for f0 in range(3):
                        wl = [u2_cast_w[f0]] if first else None
                        u2_tgt[f0] = dma("sync", ch_u2b[f0], u2b[f0, :, :],
                                         u2s_d[f0 * 128:(f0 + 1) * 128, :],
                                         wait=wl)
                        actr_tgt[f0] = dma("sync", ch_actr[f0], actr[f0, :, :],
                                           acts_d[f0, :, :],
                                           wait=[a_actdma[f0]])

            b_mm = [None] * FP
            v2_tgt = [None] * 16
            for g0 in range(2):
                v2_tgt[g0] = dma("sync", ch_v2b[g0], v2b[g0, :, :],
                                 v2s_d[g0, :, :],
                                 wait=[v2_cast_w[g0]] if first else None)
            for f in range(FP):
                par3 = f % 3
                w = [actr_tgt[f], u2_tgt[f]]
                if f == 0:
                    w.append((h_sc, a_sc[FP - 1]))
                    w.append((h_dve, a_dve[FP - 1]))
                mms = []
                for m in range(8):
                    mms.append((pb[m][:, :],
                                (lambda par3=par3, m=m:
                                 u2b[par3, :, m * 128:(m + 1) * 128]),
                                (lambda par3=par3: actr[par3, :, :]),
                                f == 0, f == FP - 1))
                b_mm[f] = mm_group(mms, wait=w, skip_check=True)
                if f + 3 < FP:
                    wl = [(h_pe, b_mm[f])]
                    if first:
                        wl.append(u2_cast_w[f + 3])
                    u2_tgt[f + 3] = dma("sync", ch_u2b[(f + 3) % 3],
                                        u2b[(f + 3) % 3, :, :],
                                        u2s_d[(f + 3) * 128:(f + 4) * 128, :],
                                        wait=wl)
                    actr_tgt[f + 3] = dma("sync", ch_actr[(f + 3) % 3],
                                          actr[(f + 3) % 3, :, :],
                                          acts_d[f + 3, :, :],
                                          wait=[a_actdma[f + 3], (h_pe, b_mm[f])])
            z_ev = []
            for m in range(8):
                z_ev.append(ins("scalar", c_sc, lambda e, m=m: e.copy(
                    ztsb[:, m * CHUNK:(m + 1) * CHUNK], pb[m][:, :]),
                    wait=[(h_pe, b_mm[FP - 1])] if m == 0 else None))
            t_zev = z_ev[-1]
            if DEBUG and ch == 0:
                dma("sync", ch_dbg, AP(dbg_d.tensor, 2048, [[8192, 128], [1, 2048]]),
                    ztsb[:, :].bitcast(F32), wait=[(h_sc, t_zev)])
                acts_f32view = AP(acts_d.tensor, 0,
                                  [[512, 128], [65536, 4], [1, 512]]).bitcast(F32)
                dma("sync", ch_dbg,
                    AP(dbg_d.tensor, 4096, [[8192, 128], [256, 4], [1, 256]]),
                    acts_f32view, wait=[(h_sc, t_zev)])

            NG2 = 16
            v2_mm = [None] * NG2
            v2_dve = [None] * DT
            for g in range(NG2):
                par = g % 2
                base = par * 2
                w = [v2_tgt[g]]
                if g == 0:
                    w.append((h_sc, t_zev))
                if g >= 2:
                    w.append((h_dve, v2_dve[(g - 2) * 2 + 1]))
                mms = []
                for k in range(8):
                    for mi in range(2):
                        mms.append((pb[base + mi][:, :],
                                    (lambda par=par, k=k, mi=mi:
                                     v2b[par, :, k * 256 + mi * 128:k * 256 + (mi + 1) * 128]),
                                    (lambda k=k: ztsb[:, k * CHUNK:(k + 1) * CHUNK]),
                                    k == 0, k == 7))
                v2_mm[g] = mm_group(mms, wait=w, skip_check=True)
                for mi in range(2):
                    m = g * 2 + mi
                    v2_dve[m] = ins("vector", c_dve, lambda e, m=m,
                                    b=pb[base + mi]: e.scalar_tensor_tensor(
                        XB(m), b[:, :], b2t_s[:, m:m + 1], XB(m),
                        ALU.add, ALU.add),
                        wait=[(h_pe, v2_mm[g]), (h_dve, ep_v4[m])])
                if g + 2 < NG2:
                    wl = [(h_pe, v2_mm[g])]
                    if first:
                        wl.append(v2_cast_w[g + 2])
                    v2_tgt[g + 2] = dma("sync", ch_v2b[par], v2b[par, :, :],
                                        v2s_d[g + 2, :, :], wait=wl)
            out_tgt = []
            for qd in range(4):
                dst = AP(out_d.tensor, (qd * 8 * 128) * TOK + tok0,
                         [[TOK, 128], [128 * TOK, 8], [1, CHUNK]])
                src = xbuf[:, qd * 8 * CHUNK:(qd + 1) * 8 * CHUNK] \
                    .rearrange("p (dt c) -> p dt c", dt=8)
                out_tgt.append(dma("sync", ch_out[qd], dst, src,
                                   wait=[(h_dve, v2_dve[qd * 8 + 7]),
                                         x_tgt[qd]]))
            prev_out = out_tgt
            prev_actr = list(actr_tgt)


        chan_totals = [(c.h, c.n) for c in _chans.values() if c.n > 0]

        @block.sync
        def _(eng):
            S.replay("sync", eng)
            for (h, v) in chan_totals:
                eng.wait_ge(h, v)

        @block.tensor
        def _(eng):
            S.replay("tensor", eng)

        @block.vector
        def _(eng):
            S.replay("vector", eng)

        @block.scalar
        def _(eng):
            S.replay("scalar", eng)

        @block.gpsimd
        def _(eng):
            S.replay("gpsimd", eng)

    return nc


_NC_CACHE = None


def kernel(L0_attn_u, L0_attn_Lambda_ri, L0_attn_p_ri, L0_attn_q_ri,
           L0_attn_Vc_ri, L0_attn_Ct_ri, L0_attn_Bp, L0_attn_D,
           L0_attn_log_step, L0_ffn_U1, L0_ffn_V1, L0_ffn_U2, L0_ffn_V2,
           L0_ffn_b1, L0_ffn_b2):
    global _NC_CACHE
    if _NC_CACHE is None:
        _NC_CACHE = build_program()
    nc = _NC_CACHE

    x = np.ascontiguousarray(np.asarray(L0_attn_u, dtype=np.float32)
                             .reshape(16384, 4096))
    vc = np.asarray(L0_attn_Vc_ri, dtype=np.float32)
    common = {
        "lam": np.ascontiguousarray(np.asarray(L0_attn_Lambda_ri, np.float32)),
        "p_ri": np.ascontiguousarray(np.asarray(L0_attn_p_ri, np.float32)),
        "q_ri": np.ascontiguousarray(np.asarray(L0_attn_q_ri, np.float32)),
        "ct_ri": np.ascontiguousarray(np.asarray(L0_attn_Ct_ri, np.float32)
                                      .reshape(64, 2)),
        "vc_reT": np.ascontiguousarray(vc[:, :, 0].T),
        "vc_imT": np.ascontiguousarray(vc[:, :, 1].T),
        "bp": np.ascontiguousarray(np.asarray(L0_attn_Bp, np.float32)),
        "dd": np.asarray(L0_attn_D, np.float32).reshape(1, 1),
        "ls": np.asarray(L0_attn_log_step, np.float32).reshape(1, 1),
        "u1": np.ascontiguousarray(np.asarray(L0_ffn_U1, np.float32)),
        "v1": np.ascontiguousarray(np.asarray(L0_ffn_V1, np.float32)),
        "u2": np.ascontiguousarray(np.asarray(L0_ffn_U2, np.float32)),
        "v2": np.ascontiguousarray(np.asarray(L0_ffn_V2, np.float32)),
        "b1t": np.ascontiguousarray(np.asarray(L0_ffn_b1, np.float32)
                                    .reshape(2 * FP, 128).T),
        "b2t": np.ascontiguousarray(np.asarray(L0_ffn_b2, np.float32)
                                    .reshape(DT, 128).T),
    }
    in_maps = []
    for c in range(N_CORES):
        m = dict(common)
        m["x_t"] = np.ascontiguousarray(x[c * TOK:(c + 1) * TOK, :].T)
        in_maps.append(m)
    res = run_bass_kernel_spmd(nc, in_maps, core_ids=list(range(N_CORES)))
    out = np.empty((16384, 4096), np.float32)
    for c in range(N_CORES):
        out[c * TOK:(c + 1) * TOK, :] = res.results[c]["out_t"].T
    return out.reshape(2, 8192, 4096)
